# version 01
# baseline (speedup 1.0000x reference)
import math
import numpy as np
import ml_dtypes
import concourse.bass as bass
import concourse.mybir as mybir
from concourse.bass_utils import run_bass_kernel_spmd

F32 = mybir.dt.float32
BF16 = mybir.dt.bfloat16
I32 = mybir.dt.int32
AF = mybir.ActivationFunctionType
ALU = mybir.AluOpType
AX = mybir.AxisListType
BF = ml_dtypes.bfloat16

D = 2048
FF = 8192
TOK = 2048
TT = 16
NQG = 4
S = 8192
NPT = 16
HD = 128
EPS = 1e-5
LAMBDA_INIT = 0.8 - 0.6 * math.exp(-0.3 * 12)
BIGM = 60.0
INV_SQRT_HD = 1.0 / math.sqrt(HD)

DBG = {}

def zeb(c4):
    return sorted([c4, 7 - c4, 8 + c4, 15 - c4])

OWN = {}
for _c in range(4):
    for _g, _j in enumerate(zeb(_c)):
        OWN[_j] = (_c, _g)


class Buf:
    __slots__ = ("t", "w", "rs", "name")

    def __init__(self, t, name=""):
        self.t = t
        self.name = name
        self.w = None
        self.rs = {}

    def __getitem__(self, idx):
        return self.t[idx]

    @property
    def ap(self):
        try:
            return self.t[:]
        except TypeError:
            return self.t.ap()


class K:
    ENGS = ("pe", "ve", "se", "sy", "gp")

    def __init__(self, nc):
        self.nc = nc
        self.eng = {"pe": nc.tensor, "ve": nc.vector, "se": nc.scalar,
                    "sy": nc.sync, "gp": nc.gpsimd}
        self.sem = {}
        self.cnt = {}
        self.hwm = {e: {} for e in self.ENGS}
        for e in self.ENGS:
            self.sem[e] = nc.alloc_semaphore("s_" + e)
            self.cnt[e] = 0
        self.sem["cc"] = nc.alloc_semaphore("s_cc")
        self.cnt["cc"] = 0
        self.bufs = []
        self._uid = 0

    def uname(self, name):
        self._uid += 1
        return f"{name}__{self._uid}"

    def buf(self, t, name=""):
        b = Buf(t, name)
        self.bufs.append(b)
        return b

    def _wait(self, ekey, deps):
        need = {}
        for (sk, v) in deps:
            if sk == "pe" and ekey == "pe":
                continue
            if v > self.hwm[ekey].get(sk, 0):
                need[sk] = max(need.get(sk, 0), v)
        for sk, v in need.items():
            self.eng[ekey].wait_ge(self.sem[sk], v)
            self.hwm[ekey][sk] = v

    def _deps(self, reads, writes):
        deps = []
        for b in reads:
            if b.w:
                deps.append(b.w)
        for b in writes:
            if b.w:
                deps.append(b.w)
            for sk, v in b.rs.items():
                deps.append((sk, v))
        return deps

    def _finish(self, ins, ekey, step, reads, writes):
        self.cnt[ekey] += step
        ins.then_inc(self.sem[ekey], step)
        tok = (ekey, self.cnt[ekey])
        for b in writes:
            b.w = tok
            b.rs = {}
        for b in reads:
            if b.rs.get(ekey, 0) < tok[1]:
                b.rs[ekey] = tok[1]
        return tok

    def op(self, ekey, fn, reads=(), writes=()):
        self._wait(ekey, self._deps(reads, writes))
        ins = fn(self.eng[ekey])
        return self._finish(ins, ekey, 1, reads, writes)

    def dma(self, qkey, out, in_, reads=(), writes=(), **kw):
        self._wait(qkey, self._deps(reads, writes))
        ins = self.eng[qkey].dma_start(out=out, in_=in_, **kw)
        return self._finish(ins, qkey, 16, reads, writes)

    def collective(self, kind, op, groups, in_b, out_b):
        self._wait("gp", self._deps([in_b], [out_b]))
        ins = self.eng["gp"].collective_compute(
            kind, op, replica_groups=groups,
            ins=[in_b.ap.opt()], outs=[out_b.ap.opt()])
        self.cnt["cc"] += 1
        ins.then_inc(self.sem["cc"], 1)
        tok = ("cc", self.cnt["cc"])
        out_b.w = tok
        out_b.rs = {}
        in_b.rs["cc"] = max(in_b.rs.get("cc", 0), tok[1])
        return tok

    def switch4(self, bodyfn):
        self.barrier()
        nc = self.nc
        pid = nc.partition_id()
        engs = [self.eng[e] for e in self.ENGS]
        idx = [eng.compute_val(pid % 4) for eng in engs]
        old_sem, old_cnt, old_hwm = self.sem, dict(self.cnt), self.hwm
        xd = nc.alloc_semaphore("s_xd")
        for i in nc.Switch(engines=engs, index=idx, n=4):
            self.sem = dict(old_sem)
            for e in self.ENGS:
                self.sem[e] = nc.alloc_semaphore(f"s_x{i}_{e}")
            self.cnt = {kk: 0 for kk in self.cnt}
            self.hwm = {e: {} for e in self.ENGS}
            for b in self.bufs:
                b.w = None
                b.rs = {}
            bodyfn(i)
            for e in self.ENGS:
                for sk in self.ENGS:
                    v = self.cnt[sk]
                    if v > self.hwm[e].get(sk, 0):
                        self.eng[e].wait_ge(self.sem[sk], v)
                        self.hwm[e][sk] = v
                self.eng[e].sem_inc(xd, 1)
        self.sem, self.cnt, self.hwm = old_sem, old_cnt, old_hwm
        for e in self.ENGS:
            self.eng[e].wait_ge(xd, 5)
        for b in self.bufs:
            b.w = None
            b.rs = {}

    def barrier(self):
        for e in self.ENGS:
            for sk in list(self.cnt):
                v = self.cnt[sk]
                if v > self.hwm[e].get(sk, 0):
                    self.eng[e].wait_ge(self.sem[sk], v)
                    self.hwm[e][sk] = v
        for b in self.bufs:
            b.w = None
            b.rs = {}

    def finish(self):
        for sk in list(self.cnt):
            v = self.cnt[sk]
            if v > self.hwm["sy"].get(sk, 0):
                self.eng["sy"].wait_ge(self.sem[sk], v)
                self.hwm["sy"][sk] = v


class Prog:
    pass


def _mk_consts(nc):
    c = Prog()
    c.ident_bf = nc.inline_tensor(np.eye(128, dtype=BF), name="ident_bf")
    c.ident_f = nc.inline_tensor(np.eye(128, dtype=np.float32), name="ident_f")
    ones_row = np.ones((1, 128), dtype=BF)
    c.ones_bf = nc.inline_tensor(ones_row, name="ones_bf")
    c.ones_f = nc.inline_tensor(np.ones((1, 128), dtype=np.float32), name="ones_f")
    c.ones_col_bf = nc.inline_tensor(np.ones((128, 1), dtype=BF), name="ones_col_bf")
    c.ones_col_f = nc.inline_tensor(np.ones((128, 1), dtype=np.float32), name="ones_col_f")
    c.one11_f = nc.inline_tensor(np.ones((1, 1), dtype=np.float32), name="one11_f")
    R = np.zeros((128, 128), dtype=np.float32)
    for j in range(64):
        R[2 * j + 1, 2 * j] = -1.0
        R[2 * j, 2 * j + 1] = 1.0
    c.rot_f = nc.inline_tensor(R, name="rot_f")
    tri = np.zeros((128, 4 * 512), dtype=np.float32)
    for di in range(4):
        kk = np.arange(128)[:, None]
        qq = np.arange(512)[None, :]
        tri[:, di * 512:(di + 1) * 512] = np.where(di * 128 + kk > qq, BIGM, 0.0)
    c.tri_f = nc.inline_tensor(tri, name="tri_f")
    return c


def emit_ln_norm(k, st, XTb, SQb, STATb, epsc):
    W = XTb.t.shape[1]
    inv = 1.0 / W
    k.op("ve", lambda e: e.reduce_sum(out=STATb[:, 0:1], in_=XTb[:], axis=AX.X),
         reads=[XTb], writes=[STATb])
    k.op("se", lambda e: e.activation(out=SQb[:], in_=XTb[:], func=AF.Square,
                                      accum_out=STATb[:, 1:2]),
         reads=[XTb], writes=[SQb, STATb])
    k.op("ve", lambda e: e.tensor_scalar(out=STATb[:, 2:3], in0=STATb[:, 0:1],
                                         scalar1=inv, scalar2=None, op0=ALU.mult),
         reads=[STATb], writes=[STATb])
    k.op("ve", lambda e: e.tensor_scalar(out=STATb[:, 3:4], in0=STATb[:, 1:2],
                                         scalar1=inv, scalar2=None, op0=ALU.mult),
         reads=[STATb], writes=[STATb])
    k.op("ve", lambda e: e.tensor_tensor(out=STATb[:, 4:5], in0=STATb[:, 2:3],
                                         in1=STATb[:, 2:3], op=ALU.mult),
         reads=[STATb], writes=[STATb])
    k.op("ve", lambda e: e.tensor_tensor(out=STATb[:, 5:6], in0=STATb[:, 3:4],
                                         in1=STATb[:, 4:5], op=ALU.subtract),
         reads=[STATb], writes=[STATb])
    k.op("se", lambda e: e.activation(out=STATb[:, 5:6], in_=STATb[:, 5:6], func=AF.Ln,
                                      bias=epsc[:, 0:1], scale=1.0),
         reads=[STATb, epsc], writes=[STATb])
    k.op("se", lambda e: e.activation(out=STATb[:, 6:7], in_=STATb[:, 5:6], func=AF.Exp,
                                      bias=0.0, scale=-0.5),
         reads=[STATb], writes=[STATb])
    k.op("ve", lambda e: e.scalar_tensor_tensor(out=STATb[:, 7:8], in0=STATb[:, 2:3],
                                                scalar=-1.0, in1=STATb[:, 6:7],
                                                op0=ALU.mult, op1=ALU.mult),
         reads=[STATb], writes=[STATb])


def emit_xnT(k, st, xsrc, XN_T, tmp, epsc, consts_sb):
    XT, SQ, STAT, XN, PTR = tmp["XT"], tmp["SQ"], tmp["STAT"], tmp["XN"], tmp["PTR"]
    IDB = consts_sb["ident_bf"]
    for t in range(TT):
        b = t % 2
        k.dma("sy", XT[b][:], xsrc.t.ap()[t * 128:(t + 1) * 128, :],
              reads=[xsrc], writes=[XT[b]])
        emit_ln_norm(k, st, XT[b], SQ, STAT[b], epsc)
        k.op("ve", lambda e, b=b: e.tensor_scalar(
            out=XN[b][:], in0=XT[b][:], scalar1=STAT[b][:, 6:7], scalar2=STAT[b][:, 7:8],
            op0=ALU.mult, op1=ALU.add), reads=[XT[b], STAT[b]], writes=[XN[b]])
        for c in range(16):
            pb = c % 2
            k.op("pe", lambda e, b=b, c=c, pb=pb: e.transpose(
                PTR[pb][:], XN[b][:, c * 128:(c + 1) * 128], IDB[:]),
                reads=[XN[b], IDB], writes=[PTR[pb]])
            k.op("ve", lambda e, t=t, c=c, pb=pb: e.tensor_copy(
                out=XN_T[:, c * 2048 + t * 128:c * 2048 + (t + 1) * 128], in_=PTR[pb][:]),
                reads=[PTR[pb]], writes=[XN_T])


def load_bias_cols(k, row_f32_dram, BCOL, consts_sb, pcol, tmprow):
    ONE11 = consts_sb["one11_f"]
    k.dma("sy", tmprow[:], row_f32_dram.t.ap(), reads=[row_f32_dram], writes=[tmprow])
    for c in range(16):
        k.op("pe", lambda e, c=c: e.matmul(
            pcol[:, c:c + 1], tmprow[0:1, c * 128:(c + 1) * 128], ONE11[:],
            start=(c == 0), stop=(c == 15)), reads=[tmprow, ONE11], writes=[pcol])
    k.op("ve", lambda e: e.tensor_copy(out=BCOL[:], in_=pcol[:, 0:16]),
         reads=[pcol], writes=[BCOL])


def build_program(core_variant_mode="switch", stop_after=None, debug_copies=()):
    nc = bass.Bass("TRN2", target_bir_lowering=False)
    cst = _mk_consts(nc)
    k = K(nc)
    st = Prog()

    dt_in = {}

    def ein(name, shape, dt=F32):
        t = nc.dram_tensor(name, list(shape), dt, kind="ExternalInput")
        b = k.buf(t, name)
        dt_in[name] = b
        return b

    x_in = ein("x_in", [TOK, D])
    W0 = {w: ein("L0_" + w, [D, D]) for w in ("Wq", "Wk", "Wv", "Wo")}
    B0 = {w: ein("L0_" + w, [1, D]) for w in ("bq", "bk", "bv", "bo")}
    w1 = ein("L0_w1", [FF, D])
    b1 = ein("L0_b1", [1, FF])
    w2 = ein("L0_w2", [D, FF])
    b2 = ein("L0_b2", [1, D])
    W1 = {w: ein("L1_" + w, [D, D]) for w in ("Wq", "Wk", "Wv", "Wo")}
    lamv = ein("L1_lamv", [1, 512])
    g1 = ein("L1_g", [1, 256])
    wupq = ein("L1_wupq", [FF, D], I32)
    wupsc = ein("L1_wupsc", [FF, 32])
    bup = ein("L1_bup", [1, FF])
    lau = ein("L1_lau", [D, 16])
    lbu = ein("L1_lbu", [16, FF])
    wdnq = ein("L1_wdnq", [D, FF], I32)
    wdnsc = ein("L1_wdnsc", [D, 128])
    bdn = ein("L1_bdn", [1, D])
    lad = ein("L1_lad", [FF, 16])
    lbd = ein("L1_lbd", [16, D])
    rcos = ein("rope_cos", [128, TOK])
    rsin = ein("rope_sin", [128, TOK])

    x_out = k.buf(nc.dram_tensor("x_out", [TOK, D], F32, kind="ExternalOutput"), "x_out")

    kv_in = k.buf(nc.dram_tensor("kv_in", [2 * TOK, D], BF16, kind="Internal"), "kv_in")
    kv_g = k.buf(nc.dram_tensor("kv_g", [4 * 2 * TOK, D], BF16, kind="Internal"), "kv_g")
    q_dram = k.buf(nc.dram_tensor("q_dram", [D, TOK], BF16, kind="Internal"), "q_dram")
    wupT = k.buf(nc.dram_tensor("wupT", [16 * 64 * 128 * 128], BF16, kind="Internal"), "wupT")
    wdnT = k.buf(nc.dram_tensor("wdnT", [64 * 16 * 128 * 128], BF16, kind="Internal"), "wdnT")
    x2t = k.buf(nc.dram_tensor("x2t", [16 * 64 * 128 * 128], BF16, kind="Internal"), "x2t")
    y2_dram = k.buf(nc.dram_tensor("y2_dram", [TOK, D], F32, kind="Internal"), "y2_dram")

    if DBG.get("b1"):
        for nm, shp, dt in (("d_sub", [128, 512], F32), ("d_ptb", [128, 1024], BF16),
                            ("d_msb", [128, 512], F32), ("d_nm", [128, 8], F32),
                            ("d_srow", [1, 1024], F32), ("d_od", [128, 256], F32),
                            ("d_kh", [128, 512], BF16), ("d_qtg", [128, 512], BF16),
                            ("d_vh", [128, 1024], BF16)):
            DBG[nm] = k.buf(nc.dram_tensor(nm, shp, dt, kind="ExternalOutput"), nm)
    dbg_bufs = {}
    for nm in debug_copies:
        src = {"kv_in": kv_in, "kv_g": kv_g, "q_dram": q_dram, "wupT": wupT,
               "wdnT": wdnT, "x2t": x2t, "y2": y2_dram, "x_out": x_out}[nm]
        dbg_bufs[nm] = (src, k.buf(nc.dram_tensor(
            "dbg_" + nm, list(src.t.shape), src.t.dtype, kind="ExternalOutput"), "dbg_" + nm))

    import contextlib
    prog_stack = contextlib.ExitStack()

    def sb(name, shape, dt=F32):
        t = prog_stack.enter_context(nc.sbuf_tensor(name, list(shape), dt))
        return k.buf(t, name)

    consts_sb = {}
    for nm, it, cdt in (("ident_bf", cst.ident_bf, BF16), ("ident_f", cst.ident_f, F32),
                        ("ones_bf", cst.ones_bf, BF16), ("ones_f", cst.ones_f, F32),
                        ("ones_col_bf", cst.ones_col_bf, BF16),
                        ("ones_col_f", cst.ones_col_f, F32),
                        ("one11_f", cst.one11_f, F32), ("rot_f", cst.rot_f, F32)):
        b = sb(nm + "_sb", list(it.shape), cdt)
        k.dma("sy", b[:], it.ap(), writes=[b])
        consts_sb[nm] = b
    epsc = sb("epsc", [128, 1])
    k.op("ve", lambda e: e.memset(epsc[:], EPS), writes=[epsc])
    qscale = sb("qscale", [128, 1])
    k.op("ve", lambda e: e.memset(qscale[:], INV_SQRT_HD), writes=[qscale])
    ln1mli = sb("ln1mli", [128, 1])
    k.op("ve", lambda e: e.memset(ln1mli[:], math.log(1.0 - LAMBDA_INIT)), writes=[ln1mli])

    brow_bf = {}
    for nm, bsrc in (("bv0", B0["bv"]), ("bo0", B0["bo"]), ("b2", b2), ("bdn", bdn)):
        b = sb("brow_" + nm, [1, bsrc.t.shape[1]], BF16)
        k.dma("gp", b[:], bsrc.t.ap(), reads=[bsrc], writes=[b])
        brow_bf[nm] = b

    def phase_A(L):
        ph = contextlib.ExitStack()

        def psb(name, shape, dt=F32):
            return k.buf(ph.enter_context(nc.sbuf_tensor(k.uname(name), list(shape), dt)), name)

        def pps(name, shape, dt=F32):
            return k.buf(ph.enter_context(nc.psum_tensor(k.uname(name), list(shape), dt)), name)

        XN_T = psb("XN_T", [128, 16 * 2048], BF16)
        tmp = {
            "XT": [psb("A_XT0", [128, 2048]), psb("A_XT1", [128, 2048])],
            "SQ": psb("A_SQ", [128, 2048]),
            "STAT": [psb("A_ST0", [128, 8]), psb("A_ST1", [128, 8])],
            "XN": [psb("A_XN0", [128, 2048], BF16), psb("A_XN1", [128, 2048], BF16)],
            "PTR": [pps("A_PT0", [128, 128], BF16), pps("A_PT1", [128, 128], BF16)],
        }
        xsrc = x_in if L == 0 else x_out
        emit_xnT(k, st, xsrc, XN_T, tmp, epsc, consts_sb)

        WG = [psb("A_WG0", [128, 16 * 512], BF16), psb("A_WG1", [128, 16 * 512], BF16)]
        psp = [pps("A_PP0", [128, 512]), pps("A_PP1", [128, 512])]
        psr = [pps("A_PR0", [128, 512]), pps("A_PR1", [128, 512])]
        QP = [psb("A_QP0", [128, 512]), psb("A_QP1", [128, 512])]
        T1 = psb("A_T1", [128, 512])
        KSTG = [psb("A_KS0", [128, 512], BF16), psb("A_KS1", [128, 512], BF16)]
        BCOL = {nm: psb("A_BC_" + nm, [128, 16]) for nm in ("bq", "bk")}
        BROW = psb("A_BROW", [1, 2048])
        pcol = pps("A_PCOL", [128, 16])
        if L == 0:
            RC = psb("A_RC", [128, TOK])
            RS = psb("A_RS", [128, TOK])
            k.dma("sy", RC[:], rcos.t.ap(), reads=[rcos], writes=[RC])
            k.dma("sy", RS[:], rsin.t.ap(), reads=[rsin], writes=[RS])
        else:
            RC = RS = None

        Wm = W0 if L == 0 else W1
        if L == 0:
            for nm in ("bq", "bk"):
                load_bias_cols(k, B0[nm], BCOL[nm], consts_sb, pcol, BROW)

        def proj(wname, out_kind, bcol=None, brow=None, scale_ap=None, do_rope=False):
            W = Wm[wname]
            wap = W.t.ap().rearrange("(c p) n -> p c n", p=128)
            for g in range(4):
                wb = g % 2
                k.dma("gp", WG[wb][:].rearrange("p (c n) -> p c n", n=512),
                      wap[:, :, g * 512:(g + 1) * 512], reads=[W], writes=[WG[wb]])
                if out_kind in ("K", "Q"):
                    for tg in range(4):
                        for dcg in range(4):
                            pb = (tg * 4 + dcg) % 2
                            dout = g * 512 + dcg * 128
                            for c in range(16):
                                k.op("pe", lambda e, c=c, wb=wb, dcg=dcg, tg=tg, pb=pb: e.matmul(
                                    psp[pb][:],
                                    WG[wb][:, c * 512 + dcg * 128:c * 512 + (dcg + 1) * 128],
                                    XN_T[:, c * 2048 + tg * 512:c * 2048 + (tg + 1) * 512],
                                    start=(c == 0), stop=(c == 15)),
                                    reads=[WG[wb], XN_T], writes=[psp[pb]])
                            dc16 = dout // 128
                            if do_rope:
                                if bcol is not None and scale_ap is not None:
                                    k.op("ve", lambda e, pb=pb, dc16=dc16: e.tensor_scalar(
                                        out=QP[pb][:], in0=psp[pb][:],
                                        scalar1=bcol[:, dc16:dc16 + 1],
                                        scalar2=scale_ap[:, 0:1], op0=ALU.add, op1=ALU.mult),
                                        reads=[psp[pb], bcol, scale_ap], writes=[QP[pb]])
                                elif bcol is not None:
                                    k.op("ve", lambda e, pb=pb, dc16=dc16: e.tensor_scalar(
                                        out=QP[pb][:], in0=psp[pb][:],
                                        scalar1=bcol[:, dc16:dc16 + 1],
                                        scalar2=None, op0=ALU.add),
                                        reads=[psp[pb], bcol], writes=[QP[pb]])
                                else:
                                    k.op("ve", lambda e, pb=pb: e.tensor_copy(
                                        out=QP[pb][:], in_=psp[pb][:]),
                                        reads=[psp[pb]], writes=[QP[pb]])
                                k.op("pe", lambda e, pb=pb: e.matmul(
                                    psr[pb][:], consts_sb["rot_f"][:], QP[pb][:],
                                    start=True, stop=True),
                                    reads=[consts_sb["rot_f"], QP[pb]], writes=[psr[pb]])
                                tgs = slice(tg * 512, (tg + 1) * 512)
                                k.op("ve", lambda e, pb=pb, tgs=tgs: e.tensor_tensor(
                                    out=T1[:], in0=QP[pb][:], in1=RC[:, tgs], op=ALU.mult),
                                    reads=[QP[pb], RC], writes=[T1])
                                k.op("ve", lambda e, pb=pb, tgs=tgs: e.tensor_tensor(
                                    out=QP[pb][:], in0=psr[pb][:], in1=RS[:, tgs], op=ALU.mult),
                                    reads=[psr[pb], RS], writes=[QP[pb]])
                                k.op("ve", lambda e, pb=pb: e.tensor_tensor(
                                    out=KSTG[pb][:], in0=T1[:], in1=QP[pb][:], op=ALU.add),
                                    reads=[T1, QP[pb]], writes=[KSTG[pb]])
                            else:
                                if scale_ap is not None:
                                    k.op("ve", lambda e, pb=pb: e.tensor_scalar(
                                        out=KSTG[pb][:], in0=psp[pb][:], scalar1=scale_ap[:, 0:1],
                                        scalar2=None, op0=ALU.mult),
                                        reads=[psp[pb], scale_ap], writes=[KSTG[pb]])
                                else:
                                    k.op("ve", lambda e, pb=pb: e.tensor_copy(
                                        out=KSTG[pb][:], in_=psp[pb][:]),
                                        reads=[psp[pb]], writes=[KSTG[pb]])
                            dst = kv_in if out_kind == "K" else q_dram
                            k.dma("sy", dst.t.ap()[dout:dout + 128, tg * 512:(tg + 1) * 512],
                                  KSTG[pb][:], reads=[KSTG[pb]], writes=[dst])
                else:
                    for t in range(TT):
                        pb = t % 2
                        last = 15 if brow is None else -1
                        for c in range(16):
                            k.op("pe", lambda e, c=c, wb=wb, t=t, pb=pb, last=last: e.matmul(
                                psp[pb][:], XN_T[:, c * 2048 + t * 128:c * 2048 + (t + 1) * 128],
                                WG[wb][:, c * 512:(c + 1) * 512],
                                start=(c == 0), stop=(c == last)),
                                reads=[WG[wb], XN_T], writes=[psp[pb]])
                        if brow is not None:
                            k.op("pe", lambda e, pb=pb, g=g: e.matmul(
                                psp[pb][:], consts_sb["ones_bf"][:],
                                brow[0:1, g * 512:(g + 1) * 512], start=False, stop=True),
                                reads=[consts_sb["ones_bf"], brow], writes=[psp[pb]])
                        k.op("se", lambda e, pb=pb: e.copy(out=KSTG[pb][:], in_=psp[pb][:]),
                             reads=[psp[pb]], writes=[KSTG[pb]])
                        k.dma("sy", kv_in.t.ap()[2048 + t * 128:2048 + (t + 1) * 128,
                                                 g * 512:(g + 1) * 512],
                              KSTG[pb][:], reads=[KSTG[pb]], writes=[kv_in])

        if L == 0:
            proj("Wk", "K", bcol=BCOL["bk"], do_rope=True)
            proj("Wv", "V", brow=brow_bf["bv0"])
        else:
            proj("Wk", "K")
            proj("Wv", "V")
        groups = [[0, 1, 2, 3], [4, 5, 6, 7]]
        k.collective("AllGather", ALU.bypass, groups, kv_in, kv_g)
        if L == 0:
            proj("Wq", "Q", bcol=BCOL["bq"], scale_ap=qscale, do_rope=True)
        else:
            proj("Wq", "Q", scale_ap=qscale)
        k.barrier()
        ph.close()

    def phase_B0(nqg=NQG, nh=16, upto="full"):
        ph = contextlib.ExitStack()

        def psb(name, shape, dt=F32):
            return k.buf(ph.enter_context(nc.sbuf_tensor(k.uname(name), list(shape), dt)), name)

        def pps(name, shape, dt=F32):
            return k.buf(ph.enter_context(nc.psum_tensor(k.uname(name), list(shape), dt)), name)

        QTG = psb("B_QTG", [128, 16 * 512], BF16)
        KH = [psb("B_KH0", [128, S], BF16)]
        VH = [psb("B_VH0", [128, 64 * 129], BF16)]
        MSB = psb("B_MSB", [128, 512])
        NM = psb("B_NM", [128, 4])
        MX = psb("B_MX", [128, 4])
        NROW = psb("B_NROW", [1, 512])
        SUB = [psb("B_SUB0", [128, 512]), psb("B_SUB1", [128, 512])]
        PTB = [psb("B_PTB0", [128, 512], BF16), psb("B_PTB1", [128, 512], BF16)]
        WOS = psb("B_WOS", [128, 16 * 2048], BF16)
        OTG = psb("B_OTG", [128, 16 * 512], BF16)
        RQ = psb("B_RQ", [128, 1])
        OFIN = psb("B_OFIN", [128, 128], BF16)
        AO = psb("B_AO", [128, 2048])
        XT = psb("B_XT", [128, 2048])
        SQ = psb("B_SQ", [128, 2048])
        STAT = psb("B_STAT", [128, 8])
        Y1 = psb("B_Y1", [128, 2048])
        pst = [pps("B_PST0", [128, 512]), pps("B_PST1", [128, 512])]
        poh = [pps(f"B_POH{i}", [128, 129]) for i in range(4)]
        pwo = [pps("B_PWO0", [128, 512]), pps("B_PWO1", [128, 512])]

        woap = W0["Wo"].t.ap().rearrange("(c p) n -> p c n", p=128)
        for cq in range(4):
            k.dma("gp", WOS[:].rearrange("p (c n) -> p c n", n=2048)[:, cq * 4:(cq + 1) * 4, :],
                  woap[:, cq * 4:(cq + 1) * 4, :], reads=[W0["Wo"]], writes=[WOS])
        k.op("ve", lambda e: e.memset(VH[0][:], 1.0), writes=[VH[0]])

        for qg in range(nqg):
            k.dma("sy", QTG[:].rearrange("p (c n) -> p c n", n=512),
                  q_dram.t.ap().rearrange("(c p) n -> p c n", p=128)[:, :, qg * 512:(qg + 1) * 512],
                  reads=[q_dram], writes=[QTG])
            for h in range(nh):
                hb = 0
                for j in range(NPT):
                    r, loc = OWN[j]
                    k.dma("sy", KH[hb][:, j * 512:(j + 1) * 512],
                          kv_g.t.ap()[r * 4096 + h * 128:r * 4096 + (h + 1) * 128,
                                      loc * 512:(loc + 1) * 512],
                          reads=[kv_g], writes=[KH[hb]])
                for j in range(NPT):
                    r, loc = OWN[j]
                    src = kv_g.t.ap()[r * 4096 + 2048 + loc * 512:
                                      r * 4096 + 2048 + (loc + 1) * 512,
                                      h * 128:(h + 1) * 128]
                    src3 = src.rearrange("(kt p) e -> p kt e", p=128)
                    dst3 = VH[hb][:].rearrange("p (kc w) -> p kc w", w=129)[:, 4 * j:4 * j + 4, 0:128]
                    k.dma("sy", dst3, src3, reads=[kv_g], writes=[VH[hb]])
                if upto == "loads":
                    continue
                for qt in range(4):
                    lhs = (h * 512 + qt * 128, h * 512 + (qt + 1) * 128)
                    for kc in range(16):
                        pb = kc % 2
                        k.op("pe", lambda e, lhs=lhs, kc=kc, pb=pb, hb=hb: e.matmul(
                            pst[pb][:], QTG[:, lhs[0]:lhs[1]],
                            KH[hb][:, kc * 512:(kc + 1) * 512], start=True, stop=True),
                            reads=[QTG, KH[hb]], writes=[pst[pb]])
                        k.op("ve", lambda e, pb=pb: e.tensor_reduce(
                            out=MX[:, 0:1] if False else MX[:, 1:2], in_=pst[pb][:],
                            axis=AX.X, op=ALU.max, negate=True),
                            reads=[pst[pb]], writes=[MX])
                        if kc == 0:
                            k.op("ve", lambda e, qt=qt: e.tensor_copy(
                                out=NM[:, qt:qt + 1], in_=MX[:, 1:2]),
                                reads=[MX], writes=[NM])
                        else:
                            k.op("ve", lambda e, qt=qt: e.tensor_tensor(
                                out=NM[:, qt:qt + 1], in0=NM[:, qt:qt + 1],
                                in1=MX[:, 1:2], op=ALU.min),
                                reads=[MX, NM], writes=[NM])
                if upto == "passA":
                    continue
                for qt in range(4):
                    k.op("pe", lambda e, qt=qt: e.matmul(
                        pst[0][0:1, qt * 128:(qt + 1) * 128], NM[:, qt:qt + 1],
                        consts_sb["ident_f"][:], start=(qt == 0), stop=(qt == 3)),
                        reads=[NM, consts_sb["ident_f"]], writes=[pst[0]])
                k.op("ve", lambda e: e.tensor_copy(out=NROW[:], in_=pst[0][0:1, 0:512]),
                     reads=[pst[0]], writes=[NROW])
                k.op("pe", lambda e: e.matmul(pst[1][:], consts_sb["ones_f"][:], NROW[:],
                                              start=True, stop=True),
                     reads=[consts_sb["ones_f"], NROW], writes=[pst[1]])
                k.op("ve", lambda e: e.tensor_copy(out=MSB[:], in_=pst[1][:]),
                     reads=[pst[1]], writes=[MSB])
                if upto == "msb":
                    continue
                for kc in range(64):
                    pb = kc % 2
                    k.op("pe", lambda e, kc=kc, pb=pb, hb=hb, h=h: e.matmul(
                        pst[pb][:], KH[hb][:, kc * 128:(kc + 1) * 128],
                        QTG[:, h * 512:(h + 1) * 512], start=True, stop=True),
                        reads=[KH[hb], QTG], writes=[pst[pb]])
                    k.op("ve", lambda e, pb=pb: e.tensor_tensor(
                        out=SUB[pb][:], in0=pst[pb][:], in1=MSB[:], op=ALU.subtract),
                        reads=[pst[pb], MSB], writes=[SUB[pb]])
                    k.op("se", lambda e, pb=pb: e.activation(
                        out=PTB[pb][:], in_=SUB[pb][:], func=AF.Exp),
                        reads=[SUB[pb]], writes=[PTB[pb]])
                    for qt in range(4):
                        k.op("pe", lambda e, kc=kc, pb=pb, qt=qt, hb=hb: e.matmul(
                            poh[qt][:], PTB[pb][:, qt * 128:(qt + 1) * 128],
                            VH[hb][:, kc * 129:(kc + 1) * 129],
                            start=(kc == 0), stop=(kc == 63)),
                            reads=[PTB[pb], VH[hb]], writes=[poh[qt]])
                if upto == "passB":
                    continue
                for qt in range(4):
                    k.op("ve", lambda e, qt=qt: e.reciprocal(
                        out=RQ[:], in_=poh[qt][:, 128:129]), reads=[poh[qt]], writes=[RQ])
                    k.op("ve", lambda e, qt=qt: e.tensor_scalar(
                        out=OFIN[:], in0=poh[qt][:, 0:128], scalar1=RQ[:, 0:1],
                        scalar2=None, op0=ALU.mult), reads=[poh[qt], RQ], writes=[OFIN])
                    trg = pst[0][:, 0:64].bitcast(BF16)
                    k.op("pe", lambda e, trg=trg: e.transpose(
                        trg, OFIN[:], consts_sb["ident_bf"][:]),
                        reads=[OFIN, consts_sb["ident_bf"]], writes=[pst[0]])
                    k.op("ve", lambda e, h=h, qt=qt, trg=trg: e.tensor_copy(
                        out=OTG[:, h * 512 + qt * 128:h * 512 + (qt + 1) * 128], in_=trg),
                        reads=[pst[0]], writes=[OTG])
            if upto != "full":
                continue
            for qt in range(4):
                for dg in range(4):
                    wb = dg % 2
                    for c in range(16):
                        k.op("pe", lambda e, c=c, qt=qt, dg=dg, wb=wb: e.matmul(
                            pwo[wb][:], OTG[:, c * 512 + qt * 128:c * 512 + (qt + 1) * 128],
                            WOS[:, c * 2048 + dg * 512:c * 2048 + (dg + 1) * 512],
                            start=(c == 0), stop=False),
                            reads=[OTG, WOS], writes=[pwo[wb]])
                    k.op("pe", lambda e, dg=dg, wb=wb: e.matmul(
                        pwo[wb][:], consts_sb["ones_bf"][:],
                        brow_bf["bo0"][0:1, dg * 512:(dg + 1) * 512], start=False, stop=True),
                        reads=[consts_sb["ones_bf"], brow_bf["bo0"]], writes=[pwo[wb]])
                    k.op("se", lambda e, dg=dg, wb=wb: e.copy(
                        out=AO[:, dg * 512:(dg + 1) * 512], in_=pwo[wb][:]),
                        reads=[pwo[wb]], writes=[AO])
                row0 = qg * 512 + qt * 128
                k.dma("sy", XT[:], x_in.t.ap()[row0:row0 + 128, :],
                      reads=[x_in], writes=[XT])
                emit_ln_norm(k, st, AO, SQ, STAT, epsc)
                k.op("ve", lambda e: e.tensor_scalar(
                    out=Y1[:], in0=AO[:], scalar1=STAT[:, 6:7], scalar2=STAT[:, 7:8],
                    op0=ALU.mult, op1=ALU.add), reads=[AO, STAT], writes=[Y1])
                k.op("ve", lambda e: e.tensor_tensor(
                    out=XT[:], in0=XT[:], in1=Y1[:], op=ALU.add),
                    reads=[Y1], writes=[XT])
                k.dma("sy", x_out.t.ap()[row0:row0 + 128, :], XT[:],
                      reads=[XT], writes=[x_out])
        k.barrier()
        ph.close()

    def phase_B1():
        ph = contextlib.ExitStack()

        def psb(name, shape, dt=F32):
            return k.buf(ph.enter_context(nc.sbuf_tensor(k.uname(name), list(shape), dt)), name)

        def pps(name, shape, dt=F32):
            return k.buf(ph.enter_context(nc.psum_tensor(k.uname(name), list(shape), dt)), name)

        QTG = psb("B1_QTG", [128, 16 * 512], BF16)
        KH = psb("B1_KH", [128, 2 * S], BF16)
        VH = psb("B1_VH", [128, 64 * 256], BF16)
        MSB = [psb("B1_MSB0", [128, 512]), psb("B1_MSB1", [128, 512])]
        NM = psb("B1_NM", [128, 8])
        MX = psb("B1_MX", [128, 1])
        NROW = [psb("B1_NR0", [1, 512]), psb("B1_NR1", [1, 512])]
        TRIS = psb("B1_TRIS", [128, 4 * 512])
        SUB = psb("B1_SUB", [128, 512])
        PTB = [psb("B1_PTB0", [128, 2 * 512], BF16), psb("B1_PTB1", [128, 2 * 512], BF16)]
        WOG = psb("B1_WOG", [128, 16 * 512], BF16)
        GCOL = psb("B1_GCOL", [128, 2])
        OTG = psb("B1_OTG", [128, 16 * 512], BF16)
        SROW = psb("B1_SROW", [1, 2 * 512])
        RC1 = psb("B1_RC1", [128, 1])
        RC2 = psb("B1_RC2", [128, 1])
        OD1 = psb("B1_OD1", [128, 256])
        OD = psb("B1_OD", [128, 256])
        SQ2 = psb("B1_SQ2", [128, 256])
        RST = psb("B1_RST", [128, 3])
        OFIN = psb("B1_OFIN", [128, 256], BF16)
        AOS = [psb("B1_AOS0", [128, 512]), psb("B1_AOS1", [128, 512])]
        LAMV = psb("B1_LAMV", [1, 512])
        TLAM = psb("B1_TLAM", [1, 128])
        E12 = psb("B1_E12", [1, 4])
        LIC = psb("B1_LIC", [1, 1])
        NEGLAM = psb("B1_NEGLAM", [128, 1])
        pst = [pps("B1_PST0", [128, 512]), pps("B1_PST1", [128, 512])]
        psO = [pps(f"B1_PO{i}", [128, 512]) for i in range(4)]
        psS = [pps("B1_PS0", [1, 512]), pps("B1_PS1", [1, 512])]

        k.dma("sy", TRIS[:], cst.tri_f.ap(), writes=[TRIS])
        k.dma("sy", LAMV[:], lamv.t.ap(), reads=[lamv], writes=[LAMV])
        for i in range(2):
            k.op("ve", lambda e, i=i: e.tensor_tensor(
                out=TLAM[:], in0=LAMV[0:1, i * 256:i * 256 + 128],
                in1=LAMV[0:1, i * 256 + 128:i * 256 + 256], op=ALU.mult),
                reads=[LAMV], writes=[TLAM])
            k.op("ve", lambda e, i=i: e.reduce_sum(
                out=E12[0:1, i:i + 1], in_=TLAM[:], axis=AX.X),
                reads=[TLAM], writes=[E12])
            k.op("se", lambda e, i=i: e.activation(
                out=E12[0:1, i:i + 1], in_=E12[0:1, i:i + 1], func=AF.Exp),
                reads=[E12], writes=[E12])
        k.op("ve", lambda e: e.memset(LIC[:], LAMBDA_INIT), writes=[LIC])
        k.op("ve", lambda e: e.tensor_tensor(
            out=E12[0:1, 2:3], in0=E12[0:1, 1:2], in1=E12[0:1, 0:1], op=ALU.subtract),
            reads=[E12], writes=[E12])
        k.op("ve", lambda e: e.tensor_tensor(
            out=E12[0:1, 3:4], in0=E12[0:1, 2:3], in1=LIC[:], op=ALU.subtract),
            reads=[E12, LIC], writes=[E12])
        k.op("pe", lambda e: e.matmul(pst[0][:, 0:1], consts_sb["ones_f"][:],
                                      E12[0:1, 3:4], start=True, stop=True),
             reads=[consts_sb["ones_f"], E12], writes=[pst[0]])
        k.op("ve", lambda e: e.tensor_copy(out=NEGLAM[:], in_=pst[0][:, 0:1]),
             reads=[pst[0]], writes=[NEGLAM])
        k.dma("sy", SROW[0:1, 0:256], g1.t.ap(), reads=[g1], writes=[SROW])
        for i in range(2):
            k.op("pe", lambda e, i=i: e.matmul(
                pst[0][:, 4 + i:5 + i], SROW[0:1, i * 128:(i + 1) * 128],
                consts_sb["one11_f"][:], start=(i == 0), stop=(i == 1)),
                reads=[SROW, consts_sb["one11_f"]], writes=[pst[0]])
        k.op("ve", lambda e: e.tensor_copy(out=GCOL[:], in_=pst[0][:, 4:6]),
             reads=[pst[0]], writes=[GCOL])

        def body(var):
            zs = zeb(var)
            for qg in range(NQG):
                z = zs[qg]
                nk5 = z + 1
                nk = 4 * z + 4
                k.dma("sy", QTG[:].rearrange("p (c n) -> p c n", n=512),
                      q_dram.t.ap().rearrange("(c p) n -> p c n", p=128)[
                          :, :, qg * 512:(qg + 1) * 512],
                      reads=[q_dram], writes=[QTG])
                for h in range(8):
                    for cc in range(2):
                        for j in range(nk5):
                            r, loc = OWN[j]
                            k.dma("sy", KH[:, cc * S + j * 512:cc * S + (j + 1) * 512],
                                  kv_g.t.ap()[r * 4096 + h * 256 + cc * 128:
                                              r * 4096 + h * 256 + (cc + 1) * 128,
                                              loc * 512:(loc + 1) * 512],
                                  reads=[kv_g], writes=[KH])
                    for j in range(nk5):
                        r, loc = OWN[j]
                        src = kv_g.t.ap()[r * 4096 + 2048 + loc * 512:
                                          r * 4096 + 2048 + (loc + 1) * 512,
                                          h * 256:(h + 1) * 256]
                        src3 = src.rearrange("(kt p) e -> p kt e", p=128)
                        dst3 = VH[:].rearrange("p (kc w) -> p kc w", w=256)[
                            :, 4 * j:4 * j + 4, :]
                        k.dma("sy", dst3, src3, reads=[kv_g], writes=[VH])
                    for cc in range(2):
                        for qt in range(4):
                            l0 = (2 * h + cc) * 512 + qt * 128
                            for kc in range(nk5):
                                pb = kc % 2
                                k.op("pe", lambda e, l0=l0, kc=kc, pb=pb, cc=cc: e.matmul(
                                    pst[pb][:], QTG[:, l0:l0 + 128],
                                    KH[:, cc * S + kc * 512:cc * S + (kc + 1) * 512],
                                    start=True, stop=True),
                                    reads=[QTG, KH], writes=[pst[pb]])
                                k.op("ve", lambda e, pb=pb: e.tensor_reduce(
                                    out=MX[:], in_=pst[pb][:], axis=AX.X,
                                    op=ALU.max, negate=True),
                                    reads=[pst[pb]], writes=[MX])
                                col = cc * 4 + qt
                                if kc == 0:
                                    k.op("ve", lambda e, col=col: e.tensor_copy(
                                        out=NM[:, col:col + 1], in_=MX[:]),
                                        reads=[MX], writes=[NM])
                                else:
                                    k.op("ve", lambda e, col=col: e.tensor_tensor(
                                        out=NM[:, col:col + 1], in0=NM[:, col:col + 1],
                                        in1=MX[:], op=ALU.min),
                                        reads=[MX], writes=[NM])
                    for cc in range(2):
                        for qt in range(4):
                            k.op("pe", lambda e, cc=cc, qt=qt: e.matmul(
                                pst[0][0:1, qt * 128:(qt + 1) * 128],
                                NM[:, cc * 4 + qt:cc * 4 + qt + 1],
                                consts_sb["ident_f"][:], start=(qt == 0), stop=(qt == 3)),
                                reads=[NM, consts_sb["ident_f"]], writes=[pst[0]])
                        k.op("ve", lambda e, cc=cc: e.tensor_copy(
                            out=NROW[cc][:], in_=pst[0][0:1, 0:512]),
                            reads=[pst[0]], writes=[NROW[cc]])
                        k.op("pe", lambda e, cc=cc: e.matmul(
                            pst[1][:], consts_sb["ones_f"][:], NROW[cc][:],
                            start=True, stop=True),
                            reads=[consts_sb["ones_f"], NROW[cc]], writes=[pst[1]])
                        k.op("ve", lambda e, cc=cc: e.tensor_copy(
                            out=MSB[cc][:], in_=pst[1][:]),
                            reads=[pst[1]], writes=[MSB[cc]])
                    for kc in range(nk):
                        pb = kc % 2
                        di = kc - 4 * z
                        for cc in range(2):
                            k.op("pe", lambda e, kc=kc, pb=pb, cc=cc, h=h: e.matmul(
                                pst[pb][:], KH[:, cc * S + kc * 128:cc * S + (kc + 1) * 128],
                                QTG[:, (2 * h + cc) * 512:(2 * h + cc + 1) * 512],
                                start=True, stop=True),
                                reads=[KH, QTG], writes=[pst[pb]])
                            k.op("ve", lambda e, pb=pb, cc=cc: e.tensor_tensor(
                                out=SUB[:], in0=pst[pb][:], in1=MSB[cc][:],
                                op=ALU.subtract),
                                reads=[pst[pb], MSB[cc]], writes=[SUB])
                            if di >= 0:
                                k.op("ve", lambda e, di=di: e.tensor_tensor(
                                    out=SUB[:], in0=SUB[:],
                                    in1=TRIS[:, di * 512:(di + 1) * 512],
                                    op=ALU.subtract),
                                    reads=[TRIS], writes=[SUB])
                            k.op("se", lambda e, pb=pb, cc=cc: e.activation(
                                out=PTB[pb][:, cc * 512:(cc + 1) * 512], in_=SUB[:],
                                func=AF.Exp),
                                reads=[SUB], writes=[PTB[pb]])
                            if (DBG.get("b1") and var == 0 and qg == 0 and h == 0
                                    and kc == 0 and cc == 0):
                                k.dma("sy", DBG["d_sub"].t.ap(), SUB[:],
                                      reads=[SUB], writes=[DBG["d_sub"]])
                                k.dma("sy", DBG["d_ptb"].t.ap(), PTB[pb][:],
                                      reads=[PTB[pb]], writes=[DBG["d_ptb"]])
                            k.op("pe", lambda e, pb=pb, cc=cc, kc=kc, nk=nk: e.matmul(
                                psS[cc][:], consts_sb["ones_col_bf"][:],
                                PTB[pb][:, cc * 512:(cc + 1) * 512],
                                start=(kc == 0), stop=(kc == nk - 1)),
                                reads=[consts_sb["ones_col_bf"], PTB[pb]], writes=[psS[cc]])
                            for qt in range(4):
                                k.op("pe", lambda e, pb=pb, cc=cc, qt=qt, kc=kc, nk=nk: e.matmul(
                                    psO[cc * 2 + qt // 2][:, (qt % 2) * 256:(qt % 2 + 1) * 256],
                                    PTB[pb][:, cc * 512 + qt * 128:cc * 512 + (qt + 1) * 128],
                                    VH[:, kc * 256:(kc + 1) * 256],
                                    start=(kc == 0 and qt % 2 == 0),
                                    stop=(kc == nk - 1)),
                                    reads=[PTB[pb], VH], writes=[psO[cc * 2 + qt // 2]])
                    if DBG.get("b1") and var == 0 and qg == 0 and h == 0:
                        k.dma("sy", DBG["d_msb"].t.ap(), MSB[0][:],
                              reads=[MSB[0]], writes=[DBG["d_msb"]])
                        k.dma("sy", DBG["d_nm"].t.ap(), NM[:], reads=[NM], writes=[DBG["d_nm"]])
                        k.dma("sy", DBG["d_kh"].t.ap(), KH[:, 0:512], reads=[KH],
                              writes=[DBG["d_kh"]])
                        k.dma("sy", DBG["d_qtg"].t.ap(), QTG[:, 0:512], reads=[QTG],
                              writes=[DBG["d_qtg"]])
                        k.dma("sy", DBG["d_vh"].t.ap(), VH[:, 0:1024], reads=[VH],
                              writes=[DBG["d_vh"]])
                    for cc in range(2):
                        k.op("ve", lambda e, cc=cc: e.tensor_copy(
                            out=SROW[0:1, cc * 512:(cc + 1) * 512], in_=psS[cc][:]),
                            reads=[psS[cc]], writes=[SROW])
                    for qt in range(4):
                        for cc in range(2):
                            k.op("pe", lambda e, cc=cc, qt=qt: e.matmul(
                                pst[0][:, 8 + cc:9 + cc],
                                SROW[0:1, cc * 512 + qt * 128:cc * 512 + (qt + 1) * 128],
                                consts_sb["one11_f"][:], start=(cc == 0), stop=(cc == 1)),
                                reads=[SROW, consts_sb["one11_f"]], writes=[pst[0]])
                        k.op("ve", lambda e: e.reciprocal(out=RC1[:], in_=pst[0][:, 8:9]),
                             reads=[pst[0]], writes=[RC1])
                        k.op("ve", lambda e: e.reciprocal(out=RC2[:], in_=pst[0][:, 9:10]),
                             reads=[pst[0]], writes=[RC2])
                        k.op("ve", lambda e: e.tensor_tensor(
                            out=RC2[:], in0=RC2[:], in1=NEGLAM[:], op=ALU.mult),
                            reads=[NEGLAM], writes=[RC2])
                        k.op("ve", lambda e, qt=qt: e.tensor_scalar(
                            out=OD1[:], in0=psO[qt // 2][:, (qt % 2) * 256:(qt % 2 + 1) * 256],
                            scalar1=RC1[:, 0:1], scalar2=None, op0=ALU.mult),
                            reads=[psO[qt // 2], RC1], writes=[OD1])
                        k.op("ve", lambda e, qt=qt: e.scalar_tensor_tensor(
                            out=OD[:], in0=psO[2 + qt // 2][:, (qt % 2) * 256:(qt % 2 + 1) * 256],
                            scalar=RC2[:, 0:1], in1=OD1[:], op0=ALU.mult, op1=ALU.add),
                            reads=[psO[2 + qt // 2], RC2, OD1], writes=[OD])
                        if (DBG.get("b1") and var == 0 and qg == 0 and h == 0 and qt == 0):
                            k.dma("sy", DBG["d_od"].t.ap(), OD[:], reads=[OD],
                                  writes=[DBG["d_od"]])
                            k.dma("sy", DBG["d_srow"].t.ap(), SROW[:], reads=[SROW],
                                  writes=[DBG["d_srow"]])
                        k.op("se", lambda e: e.activation(
                            out=SQ2[:], in_=OD[:], func=AF.Square,
                            accum_out=RST[:, 0:1]),
                            reads=[OD], writes=[SQ2, RST])
                        k.op("ve", lambda e: e.tensor_scalar(
                            out=RST[:, 1:2], in0=RST[:, 0:1], scalar1=1.0 / 256,
                            scalar2=epsc[:, 0:1], op0=ALU.mult, op1=ALU.add),
                            reads=[RST, epsc], writes=[RST])
                        k.op("se", lambda e: e.activation(
                            out=RST[:, 1:2], in_=RST[:, 1:2], func=AF.Ln),
                            reads=[RST], writes=[RST])
                        k.op("se", lambda e: e.activation(
                            out=RST[:, 2:3], in_=RST[:, 1:2], func=AF.Exp,
                            bias=ln1mli[:, 0:1], scale=-0.5),
                            reads=[RST, ln1mli], writes=[RST])
                        k.op("ve", lambda e: e.tensor_scalar(
                            out=OFIN[:], in0=OD[:], scalar1=RST[:, 2:3], scalar2=None,
                            op0=ALU.mult), reads=[OD, RST], writes=[OFIN])
                        for half in range(2):
                            trg = pst[1][:, 0:64].bitcast(BF16)
                            k.op("pe", lambda e, half=half, trg=trg: e.transpose(
                                trg, OFIN[:, half * 128:(half + 1) * 128],
                                consts_sb["ident_bf"][:]),
                                reads=[OFIN, consts_sb["ident_bf"]], writes=[pst[1]])
                            k.op("ve", lambda e, h=h, half=half, qt=qt, trg=trg: e.tensor_copy(
                                out=OTG[:, (2 * h + half) * 512 + qt * 128:
                                        (2 * h + half) * 512 + (qt + 1) * 128], in_=trg),
                                reads=[pst[1]], writes=[OTG])
                for dg in range(4):
                    wosrc = W1["Wo"].t.ap().rearrange("(c p) n -> p c n", p=128)[
                        :, :, dg * 512:(dg + 1) * 512]
                    k.dma("gp", WOG[:].rearrange("p (c n) -> p c n", n=512), wosrc,
                          reads=[W1["Wo"]], writes=[WOG])
                    for c in range(16):
                        k.op("ve", lambda e, c=c: e.tensor_scalar(
                            out=WOG[:, c * 512:(c + 1) * 512],
                            in0=WOG[:, c * 512:(c + 1) * 512],
                            scalar1=GCOL[:, c % 2:c % 2 + 1], scalar2=None,
                            op0=ALU.mult), reads=[GCOL], writes=[WOG])
                    for qt in range(4):
                        pb = qt % 2
                        for c in range(16):
                            k.op("pe", lambda e, c=c, qt=qt, pb=pb: e.matmul(
                                pst[pb][:], OTG[:, c * 512 + qt * 128:c * 512 + (qt + 1) * 128],
                                WOG[:, c * 512:(c + 1) * 512],
                                start=(c == 0), stop=(c == 15)),
                                reads=[OTG, WOG], writes=[pst[pb]])
                        k.op("se", lambda e, pb=pb: e.copy(
                            out=AOS[pb][:], in_=pst[pb][:]), reads=[pst[pb]], writes=[AOS[pb]])
                        row0 = qg * 512 + qt * 128
                        k.dma("sy", y2_dram.t.ap()[row0:row0 + 128, dg * 512:(dg + 1) * 512],
                              AOS[pb][:], reads=[AOS[pb]], writes=[y2_dram])

        k.switch4(body)
        ph.close()
        ph2 = contextlib.ExitStack()

        def psb2(name, shape, dt=F32):
            return k.buf(ph2.enter_context(nc.sbuf_tensor(k.uname(name), list(shape), dt)), name)

        Y2T = [psb2("B1_Y2T0", [128, 2048]), psb2("B1_Y2T1", [128, 2048])]
        XT = [psb2("B1_XT0", [128, 2048]), psb2("B1_XT1", [128, 2048])]
        SQ = psb2("B1_SQe", [128, 2048])
        STAT = [psb2("B1_STe0", [128, 8]), psb2("B1_STe1", [128, 8])]
        Y1 = [psb2("B1_Y1e0", [128, 2048]), psb2("B1_Y1e1", [128, 2048])]
        for t in range(TT):
            b = t % 2
            k.dma("sy", Y2T[b][:], y2_dram.t.ap()[t * 128:(t + 1) * 128, :],
                  reads=[y2_dram], writes=[Y2T[b]])
            k.dma("sy", XT[b][:], x_out.t.ap()[t * 128:(t + 1) * 128, :],
                  reads=[x_out], writes=[XT[b]])
            emit_ln_norm(k, st, Y2T[b], SQ, STAT[b], epsc)
            k.op("ve", lambda e, b=b: e.tensor_scalar(
                out=Y1[b][:], in0=Y2T[b][:], scalar1=STAT[b][:, 6:7],
                scalar2=STAT[b][:, 7:8], op0=ALU.mult, op1=ALU.add),
                reads=[Y2T[b], STAT[b]], writes=[Y1[b]])
            k.op("ve", lambda e, b=b: e.tensor_tensor(
                out=XT[b][:], in0=XT[b][:], in1=Y1[b][:], op=ALU.add),
                reads=[Y1[b]], writes=[XT[b]])
            k.dma("sy", x_out.t.ap()[t * 128:(t + 1) * 128, :], XT[b][:],
                  reads=[XT[b]], writes=[x_out])
        k.barrier()
        ph2.close()

    def phase_prep(L):
        ph = contextlib.ExitStack()

        def psb(name, shape, dt=F32):
            return k.buf(ph.enter_context(nc.sbuf_tensor(k.uname(name), list(shape), dt)), name)

        def pps(name, shape, dt=F32):
            return k.buf(ph.enter_context(nc.psum_tensor(k.uname(name), list(shape), dt)), name)

        ps_tr = [pps("P_TR0", [128, 128], BF16), pps("P_TR1", [128, 128], BF16)]
        ps11 = pps("P_11", [1, 1])
        pcol = pps("P_COL", [128, 1])

        def quant_matrix(wsrc, nrow, ncol, dstT, dst_layout, px):
            nt = nrow // 128
            WT = [psb(px + "_WT0", [128, ncol]), psb(px + "_WT1", [128, ncol])]
            ACC = psb(px + "_ACC", [128, 1])
            TCOL = psb(px + "_TCOL", [128, 1])
            TAU = psb(px + "_TAU", [1, 1])
            TFACT = psb(px + "_TFACT", [1, 1])
            TAUC = psb(px + "_TAUC", [128, 1])
            NTAUC = psb(px + "_NTAUC", [128, 1])
            POS = psb(px + "_POS", [128, ncol], BF16)
            NEG = psb(px + "_NEG", [128, ncol], BF16)
            WQB = psb(px + "_WQB", [128, ncol], BF16)
            TSTG = psb(px + "_TSTG", [128, ncol], BF16)
            k.op("ve", lambda e: e.memset(ACC[:], 0.0), writes=[ACC])
            k.op("ve", lambda e: e.memset(TFACT[:], 0.7 / (nrow * ncol)), writes=[TFACT])
            wap = wsrc.t.ap()
            for t in range(nt):
                b = t % 2
                k.dma("sy", WT[b][:], wap[t * 128:(t + 1) * 128, :],
                      reads=[wsrc], writes=[WT[b]])
                k.op("ve", lambda e, b=b: e.reduce_sum(
                    out=TCOL[:], in_=WT[b][:], axis=AX.X, apply_absolute_value=True),
                    reads=[WT[b]], writes=[TCOL])
                k.op("ve", lambda e: e.tensor_tensor(
                    out=ACC[:], in0=ACC[:], in1=TCOL[:], op=ALU.add),
                    reads=[TCOL], writes=[ACC])
            k.op("pe", lambda e: e.matmul(ps11[:], ACC[:], consts_sb["ones_col_f"][:],
                                          start=True, stop=True),
                 reads=[ACC, consts_sb["ones_col_f"]], writes=[ps11])
            k.op("ve", lambda e: e.tensor_tensor(out=TAU[:], in0=ps11[:], in1=TFACT[:],
                                                 op=ALU.mult),
                 reads=[ps11, TFACT], writes=[TAU])
            k.op("pe", lambda e: e.matmul(pcol[:], consts_sb["ones_f"][:], TAU[:],
                                          start=True, stop=True),
                 reads=[consts_sb["ones_f"], TAU], writes=[pcol])
            k.op("ve", lambda e: e.tensor_copy(out=TAUC[:], in_=pcol[:]),
                 reads=[pcol], writes=[TAUC])
            k.op("ve", lambda e: e.tensor_scalar(out=NTAUC[:], in0=TAUC[:], scalar1=-1.0,
                                                 scalar2=None, op0=ALU.mult),
                 reads=[TAUC], writes=[NTAUC])
            nc_ = ncol // 128
            for t in range(nt):
                b = t % 2
                k.dma("sy", WT[b][:], wap[t * 128:(t + 1) * 128, :],
                      reads=[wsrc], writes=[WT[b]])
                k.op("ve", lambda e, b=b: e.tensor_scalar(
                    out=POS[:], in0=WT[b][:], scalar1=TAUC[:, 0:1], scalar2=None,
                    op0=ALU.is_ge), reads=[WT[b], TAUC], writes=[POS])
                k.op("ve", lambda e, b=b: e.tensor_scalar(
                    out=NEG[:], in0=WT[b][:], scalar1=NTAUC[:, 0:1], scalar2=None,
                    op0=ALU.is_le), reads=[WT[b], NTAUC], writes=[NEG])
                k.op("ve", lambda e: e.tensor_tensor(
                    out=WQB[:], in0=POS[:], in1=NEG[:], op=ALU.subtract),
                    reads=[POS, NEG], writes=[WQB])
                _transpose_out(t, WQB, TSTG, nc_, dstT, dst_layout, nt)
            return

        def _transpose_out(t, WQB, TSTG, nchunks, dstT, dst_layout, nt):
            for c in range(nchunks):
                pb = c % 2
                k.op("pe", lambda e, c=c, pb=pb: e.transpose(
                    ps_tr[pb][:], WQB[:, c * 128:(c + 1) * 128], consts_sb["ident_bf"][:]),
                    reads=[WQB, consts_sb["ident_bf"]], writes=[ps_tr[pb]])
                k.op("ve", lambda e, c=c, pb=pb: e.tensor_copy(
                    out=TSTG[:, c * 128:(c + 1) * 128], in_=ps_tr[pb][:]),
                    reads=[ps_tr[pb]], writes=[TSTG])
            if dst_layout == "up":
                dst = dstT.t.ap().rearrange("(dc rest) -> dc rest", rest=64 * 16384)[
                    :, t * 16384:(t + 1) * 16384].rearrange("dc (p n) -> p dc n", n=128)
                src = TSTG[:].rearrange("p (dc n) -> p dc n", n=128)
            else:
                dst = dstT.t.ap().rearrange("(fc rest) -> fc rest", rest=16 * 16384)[
                    :, t * 16384:(t + 1) * 16384].rearrange("fc (p n) -> p fc n", n=128)
                src = TSTG[:].rearrange("p (fc n) -> p fc n", n=128)
            k.dma("sy", dst, src, reads=[TSTG], writes=[dstT])

        def dequant_matrix(qsrc, scsrc, nrow, ncol, dstT, dst_layout, px):
            nt = nrow // 128
            ng = ncol // 64
            WI = [psb(px + "_WI0", [128, ncol], I32), psb(px + "_WI1", [128, ncol], I32)]
            WF = psb(px + "_WF", [128, ncol])
            SC = [psb(px + "_SC0", [128, ng]), psb(px + "_SC1", [128, ng])]
            WQB = psb(px + "_WQB", [128, ncol], BF16)
            TSTG = psb(px + "_TSTG", [128, ncol], BF16)
            qap = qsrc.t.ap()
            sap = scsrc.t.ap()
            for t in range(nt):
                b = t % 2
                k.dma("sy", WI[b][:], qap[t * 128:(t + 1) * 128, :],
                      reads=[qsrc], writes=[WI[b]])
                k.dma("sy", SC[b][:], sap[t * 128:(t + 1) * 128, :],
                      reads=[scsrc], writes=[SC[b]])
                k.op("se", lambda e, b=b: e.activation(
                    out=WF[:], in_=WI[b][:], func=AF.Copy, bias=-7.5, scale=1.0),
                    reads=[WI[b]], writes=[WF])
                k.op("ve", lambda e, b=b: e.tensor_tensor(
                    out=WQB[:].rearrange("p (g o) -> p g o", o=64),
                    in0=WF[:].rearrange("p (g o) -> p g o", o=64),
                    in1=SC[b][:].rearrange("p (g o) -> p g o", o=1).broadcast_to(
                        [128, ng, 64]),
                    op=ALU.mult), reads=[WF, SC[b]], writes=[WQB])
                _transpose_out(t, WQB, TSTG, ncol // 128, dstT, dst_layout, nt)

        if L == 0:
            quant_matrix(w1, FF, D, wupT, "up", "Pa")
            quant_matrix(w2, D, FF, wdnT, "dn", "Pb")
        else:
            dequant_matrix(wupq, wupsc, FF, D, wupT, "up", "Pa")
            dequant_matrix(wdnq, wdnsc, D, FF, wdnT, "dn", "Pb")
        k.barrier()
        ph.close()

    def phase_C(L):
        ph = contextlib.ExitStack()

        def psb(name, shape, dt=F32):
            return k.buf(ph.enter_context(nc.sbuf_tensor(k.uname(name), list(shape), dt)), name)

        def pps(name, shape, dt=F32):
            return k.buf(ph.enter_context(nc.psum_tensor(k.uname(name), list(shape), dt)), name)

        XN_T = psb("C_XN_T", [128, 16 * 2048], BF16)
        tmp = {
            "XT": [psb("C_XT0", [128, 2048]), psb("C_XT1", [128, 2048])],
            "SQ": psb("C_SQ", [128, 2048]),
            "STAT": [psb("C_ST0", [128, 8]), psb("C_ST1", [128, 8])],
            "XN": [psb("C_XN0", [128, 2048], BF16), psb("C_XN1", [128, 2048], BF16)],
            "PTR": [pps("C_PT0", [128, 128], BF16), pps("C_PT1", [128, 128], BF16)],
        }
        emit_xnT(k, st, x_out, XN_T, tmp, epsc, consts_sb)

        W1G = [psb("C_W1G0", [128, 16 * 512], BF16), psb("C_W1G1", [128, 16 * 512], BF16)]
        BUPR = psb("C_BUPR", [1, FF], BF16)
        X2S = [psb("C_X2S0", [128, 512], BF16), psb("C_X2S1", [128, 512], BF16)]
        X2TS = [psb("C_X2TS0", [128, 512], BF16), psb("C_X2TS1", [128, 512], BF16)]
        pu = [pps("C_PU0", [128, 512]), pps("C_PU1", [128, 512])]
        ps_tr = [pps("C_TR0", [128, 128], BF16), pps("C_TR1", [128, 128], BF16)]
        k.dma("gp", BUPR[:], (b1 if L == 0 else bup).t.ap(),
              reads=[b1 if L == 0 else bup], writes=[BUPR])
        if L == 1:
            LAU = psb("C_LAU", [128, 256], BF16)
            LBU = psb("C_LBU", [16, FF], BF16)
            TAT = psb("C_TAT", [16, 2048], BF16)
            pta = pps("C_PTA", [16, 512])
            k.dma("gp", LAU[:].rearrange("p (c n) -> p c n", n=16),
                  lau.t.ap().rearrange("(c p) n -> p c n", p=128), reads=[lau], writes=[LAU])
            k.dma("gp", LBU[:], lbu.t.ap(), reads=[lbu], writes=[LBU])
            for tg in range(4):
                for c in range(16):
                    k.op("pe", lambda e, c=c, tg=tg: e.matmul(
                        pta[:], LAU[:, c * 16:(c + 1) * 16],
                        XN_T[:, c * 2048 + tg * 512:c * 2048 + (tg + 1) * 512],
                        start=(c == 0), stop=(c == 15)),
                        reads=[LAU, XN_T], writes=[pta])
                k.op("ve", lambda e, tg=tg: e.tensor_copy(
                    out=TAT[0:16, tg * 512:(tg + 1) * 512], in_=pta[:]),
                    reads=[pta], writes=[TAT])
        act_fn = AF.Gelu if L == 0 else AF.Relu
        for f in range(16):
            wb = f % 2
            srcj = wupT.t.ap().rearrange("(dc rest) -> dc rest", rest=64 * 16384)[
                :, (4 * f) * 16384:(4 * f + 4) * 16384].rearrange(
                "dc (j p n) -> p dc j n", p=128, n=128)
            w1gv = W1G[wb][:].rearrange("p (dc jn) -> p dc jn", jn=512)
            for j in range(4):
                k.dma("sy", w1gv[:, :, j * 128:(j + 1) * 128], srcj[:, :, j, :],
                      reads=[wupT], writes=[W1G[wb]])
            for t in range(TT):
                pb = t % 2
                for c in range(16):
                    k.op("pe", lambda e, c=c, t=t, pb=pb, wb=wb: e.matmul(
                        pu[pb][:], XN_T[:, c * 2048 + t * 128:c * 2048 + (t + 1) * 128],
                        W1G[wb][:, c * 512:(c + 1) * 512],
                        start=(c == 0), stop=False),
                        reads=[XN_T, W1G[wb]], writes=[pu[pb]])
                if L == 1:
                    k.op("pe", lambda e, t=t, pb=pb, f=f: e.matmul(
                        pu[pb][:], TAT[0:16, t * 128:(t + 1) * 128],
                        LBU[0:16, f * 512:(f + 1) * 512], start=False, stop=False),
                        reads=[TAT, LBU], writes=[pu[pb]])
                k.op("pe", lambda e, pb=pb, f=f: e.matmul(
                    pu[pb][:], consts_sb["ones_bf"][:], BUPR[0:1, f * 512:(f + 1) * 512],
                    start=False, stop=True),
                    reads=[consts_sb["ones_bf"], BUPR], writes=[pu[pb]])
                k.op("se", lambda e, pb=pb: e.activation(
                    out=X2S[pb][:], in_=pu[pb][:], func=act_fn),
                    reads=[pu[pb]], writes=[X2S[pb]])
                for j in range(4):
                    tb = j % 2
                    k.op("pe", lambda e, j=j, pb=pb, tb=tb: e.transpose(
                        ps_tr[tb][:], X2S[pb][:, j * 128:(j + 1) * 128],
                        consts_sb["ident_bf"][:]),
                        reads=[X2S[pb], consts_sb["ident_bf"]], writes=[ps_tr[tb]])
                    k.op("ve", lambda e, j=j, pb=pb, tb=tb: e.tensor_copy(
                        out=X2TS[pb][:, j * 128:(j + 1) * 128], in_=ps_tr[tb][:]),
                        reads=[ps_tr[tb]], writes=[X2TS[pb]])
                dst = x2t.t.ap()[(t * 64 + 4 * f) * 16384:(t * 64 + 4 * f + 4) * 16384
                                 ].rearrange("(j p n) -> p j n", p=128, n=128)
                k.dma("sy", dst, X2TS[pb][:].rearrange("p (j n) -> p j n", n=128),
                      reads=[X2TS[pb]], writes=[x2t])
        k.barrier()
        ph.close()

    def phase_D(L):
        ph = contextlib.ExitStack()

        def psb(name, shape, dt=F32):
            return k.buf(ph.enter_context(nc.sbuf_tensor(k.uname(name), list(shape), dt)), name)

        def pps(name, shape, dt=F32):
            return k.buf(ph.enter_context(nc.psum_tensor(k.uname(name), list(shape), dt)), name)

        W2G = psb("D_W2G", [128, 64 * 512], BF16)
        X2TSB = [psb("D_X2T0", [128, FF], BF16), psb("D_X2T1", [128, FF], BF16)]
        Y2S = [psb("D_Y2S0", [128, 512]), psb("D_Y2S1", [128, 512])]
        BDNR = brow_bf["b2" if L == 0 else "bdn"]
        pd = [pps("D_PD0", [128, 512]), pps("D_PD1", [128, 512])]
        if L == 1:
            LAD = psb("D_LAD", [128, 1024], BF16)
            LBD = psb("D_LBD", [16, 2048], BF16)
            TDT = psb("D_TDT", [16, 2048], BF16)
            ptd = pps("D_PTD", [16, 128])
            k.dma("gp", LAD[:].rearrange("p (c n) -> p c n", n=16),
                  lad.t.ap().rearrange("(c p) n -> p c n", p=128), reads=[lad], writes=[LAD])
            k.dma("gp", LBD[:], lbd.t.ap(), reads=[lbd], writes=[LBD])
        for dg in range(4):
            wsrc = wdnT.t.ap().rearrange("(fc rest) -> fc rest", rest=16 * 16384)[
                :, (4 * dg) * 16384:(4 * dg + 4) * 16384].rearrange(
                "fc (j p n) -> p fc j n", p=128, n=128)
            w2gv = W2G[:].rearrange("p (fc jn) -> p fc jn", jn=512)
            for q4 in range(4):
                for j in range(4):
                    k.dma("sy", w2gv[:, q4 * 16:(q4 + 1) * 16, j * 128:(j + 1) * 128],
                          wsrc[:, q4 * 16:(q4 + 1) * 16, j, :], reads=[wdnT], writes=[W2G])
            for t in range(TT):
                pb = t % 2
                k.dma("sy", X2TSB[pb][:].rearrange("p (fc n) -> p fc n", n=128),
                      x2t.t.ap()[t * 64 * 16384:(t + 1) * 64 * 16384].rearrange(
                          "(fc p n) -> p fc n", p=128, n=128),
                      reads=[x2t], writes=[X2TSB[pb]])
                if L == 1 and dg == 0:
                    for fc in range(64):
                        k.op("pe", lambda e, fc=fc, pb=pb: e.matmul(
                            ptd[:], LAD[:, fc * 16:(fc + 1) * 16],
                            X2TSB[pb][:, fc * 128:(fc + 1) * 128],
                            start=(fc == 0), stop=(fc == 63)),
                            reads=[LAD, X2TSB[pb]], writes=[ptd])
                    k.op("ve", lambda e, t=t: e.tensor_copy(
                        out=TDT[0:16, t * 128:(t + 1) * 128], in_=ptd[:]),
                        reads=[ptd], writes=[TDT])
                for fc in range(64):
                    k.op("pe", lambda e, fc=fc, t=t, pb=pb, dg=dg: e.matmul(
                        pd[pb][:], X2TSB[pb][:, fc * 128:(fc + 1) * 128],
                        W2G[:, fc * 512:(fc + 1) * 512],
                        start=(fc == 0), stop=False),
                        reads=[X2TSB[pb], W2G], writes=[pd[pb]])
                if L == 1:
                    k.op("pe", lambda e, t=t, pb=pb, dg=dg: e.matmul(
                        pd[pb][:], TDT[0:16, t * 128:(t + 1) * 128],
                        LBD[0:16, dg * 512:(dg + 1) * 512], start=False, stop=False),
                        reads=[TDT, LBD], writes=[pd[pb]])
                k.op("pe", lambda e, pb=pb, dg=dg: e.matmul(
                    pd[pb][:], consts_sb["ones_bf"][:], BDNR[0:1, dg * 512:(dg + 1) * 512],
                    start=False, stop=True),
                    reads=[consts_sb["ones_bf"], BDNR], writes=[pd[pb]])
                k.op("se", lambda e, pb=pb: e.copy(out=Y2S[pb][:], in_=pd[pb][:]),
                     reads=[pd[pb]], writes=[Y2S[pb]])
                k.dma("sy", y2_dram.t.ap()[t * 128:(t + 1) * 128, dg * 512:(dg + 1) * 512],
                      Y2S[pb][:], reads=[Y2S[pb]], writes=[y2_dram])
        Y2T = [psb("D_Y2T0", [128, 2048]), psb("D_Y2T1", [128, 2048])]
        XT = [psb("D_XT0", [128, 2048]), psb("D_XT1", [128, 2048])]
        SQ = psb("D_SQ", [128, 2048])
        STAT = [psb("D_ST0", [128, 8]), psb("D_ST1", [128, 8])]
        Y1 = [psb("D_Y10", [128, 2048]), psb("D_Y11", [128, 2048])]
        for t in range(TT):
            b = t % 2
            k.dma("sy", Y2T[b][:], y2_dram.t.ap()[t * 128:(t + 1) * 128, :],
                  reads=[y2_dram], writes=[Y2T[b]])
            k.dma("sy", XT[b][:], x_out.t.ap()[t * 128:(t + 1) * 128, :],
                  reads=[x_out], writes=[XT[b]])
            emit_ln_norm(k, st, Y2T[b], SQ, STAT[b], epsc)
            k.op("ve", lambda e, b=b: e.tensor_scalar(
                out=Y1[b][:], in0=Y2T[b][:], scalar1=STAT[b][:, 6:7],
                scalar2=STAT[b][:, 7:8], op0=ALU.mult, op1=ALU.add),
                reads=[Y2T[b], STAT[b]], writes=[Y1[b]])
            k.op("ve", lambda e, b=b: e.tensor_tensor(
                out=XT[b][:], in0=XT[b][:], in1=Y1[b][:], op=ALU.add),
                reads=[Y1[b]], writes=[XT[b]])
            k.dma("sy", x_out.t.ap()[t * 128:(t + 1) * 128, :], XT[b][:],
                  reads=[XT[b]], writes=[x_out])
        k.barrier()
        ph.close()

    phase_A(0)
    if stop_after == "L0A":
        pass
    elif stop_after is not None and stop_after.startswith("L0Bsub:"):
        _, spec = stop_after.split(":", 1)
        qn, hn, up = spec.split(",")
        phase_B0(nqg=int(qn), nh=int(hn), upto=up)
    else:
        phase_B0()
        if stop_after != "L0B":
            phase_prep(0)
            phase_C(0)
            phase_D(0)
            if stop_after != "L0D":
                phase_A(1)
                phase_B1()
                if stop_after != "L1B":
                    phase_prep(1)
                    phase_C(1)
                    phase_D(1)

    for nm, (src, dst) in dbg_bufs.items():
        k.dma("sy", dst.t.ap(), src.t.ap(), reads=[src], writes=[dst])
    k.finish()
    prog_stack.close()
    return nc


def _rope_tables(positions):
    j = np.arange(64, dtype=np.float32)
    inv_freq = np.power(np.float32(10000.0), (-2.0 * j / 128.0).astype(np.float32))
    pos = positions.astype(np.float32)
    ang = pos[:, None] * inv_freq[None, :]
    sin = np.sin(ang).astype(np.float32)
    cos = np.cos(ang).astype(np.float32)
    sin2 = np.stack([sin, sin], axis=-1).reshape(len(positions), 128)
    cos2 = np.stack([cos, cos], axis=-1).reshape(len(positions), 128)
    return cos2.T.copy(), sin2.T.copy()


def core_positions(c):
    c4 = c % 4
    out = []
    for jz in zeb(c4):
        out.extend(range(jz * 512, (jz + 1) * 512))
    return np.array(out, dtype=np.int64)


def kernel(**inputs):
    nc = build_program()
    in_maps = []
    for c in range(8):
        b = c // 4
        pos = core_positions(c)
        cos, sin = _rope_tables(pos)
        m = {
            "x_in": np.ascontiguousarray(inputs["L0_attn_hidden_state"][b, pos, :]),
            "rope_cos": cos, "rope_sin": sin,
            "L0_Wq": inputs["L0_attn_Wq"], "L0_Wk": inputs["L0_attn_Wk"],
            "L0_Wv": inputs["L0_attn_Wv"], "L0_Wo": inputs["L0_attn_Wo"],
            "L0_bq": inputs["L0_attn_bq"].reshape(1, -1),
            "L0_bk": inputs["L0_attn_bk"].reshape(1, -1),
            "L0_bv": inputs["L0_attn_bv"].reshape(1, -1),
            "L0_bo": inputs["L0_attn_bo"].reshape(1, -1),
            "L0_w1": inputs["L0_ffn_w1"], "L0_b1": inputs["L0_ffn_b1"].reshape(1, -1),
            "L0_w2": inputs["L0_ffn_w2"], "L0_b2": inputs["L0_ffn_b2"].reshape(1, -1),
            "L1_Wq": inputs["L1_attn_Wq"], "L1_Wk": inputs["L1_attn_Wk"],
            "L1_Wv": inputs["L1_attn_Wv"], "L1_Wo": inputs["L1_attn_Wo"],
            "L1_lamv": np.concatenate([
                inputs["L1_attn_lambda_q1"], inputs["L1_attn_lambda_k1"],
                inputs["L1_attn_lambda_q2"], inputs["L1_attn_lambda_k2"]]).reshape(1, -1),
            "L1_g": inputs["L1_attn_g"].reshape(1, -1),
            "L1_wupq": inputs["L1_ffn_w_up_q"], "L1_wupsc": inputs["L1_ffn_w_up_scale"],
            "L1_bup": inputs["L1_ffn_b_up"].reshape(1, -1),
            "L1_lau": inputs["L1_ffn_w_up_lora_a"], "L1_lbu": inputs["L1_ffn_w_up_lora_b"],
            "L1_wdnq": inputs["L1_ffn_w_down_q"], "L1_wdnsc": inputs["L1_ffn_w_down_scale"],
            "L1_bdn": inputs["L1_ffn_b_down"].reshape(1, -1),
            "L1_lad": inputs["L1_ffn_w_down_lora_a"], "L1_lbd": inputs["L1_ffn_w_down_lora_b"],
        }
        in_maps.append({kk: np.ascontiguousarray(np.asarray(vv)) for kk, vv in m.items()})
    res = run_bass_kernel_spmd(nc, in_maps, core_ids=list(range(8)))
    out = np.zeros((2, S, D), dtype=np.float32)
    for c in range(8):
        out[c // 4, core_positions(c), :] = res.results[c]["x_out"]
    return out


# version 02
# speedup vs baseline: 1.1442x; 1.1442x over previous
import math
import numpy as np
import ml_dtypes
import concourse.bass as bass
import concourse.mybir as mybir
from concourse.bass_utils import run_bass_kernel_spmd

F32 = mybir.dt.float32
BF16 = mybir.dt.bfloat16
I32 = mybir.dt.int32
AF = mybir.ActivationFunctionType
ALU = mybir.AluOpType
AX = mybir.AxisListType
BF = ml_dtypes.bfloat16

D = 2048
FF = 8192
TOK = 2048
TT = 16
NQG = 4
S = 8192
NPT = 16
HD = 128
EPS = 1e-5
LAMBDA_INIT = 0.8 - 0.6 * math.exp(-0.3 * 12)
BIGM = 60.0
INV_SQRT_HD = 1.0 / math.sqrt(HD)

DBG = {}

def zeb(c4):
    return sorted([c4, 7 - c4, 8 + c4, 15 - c4])

OWN = {}
for _c in range(4):
    for _g, _j in enumerate(zeb(_c)):
        OWN[_j] = (_c, _g)


class Buf:
    __slots__ = ("t", "w", "rs", "name")

    def __init__(self, t, name=""):
        self.t = t
        self.name = name
        self.w = None
        self.rs = {}

    def __getitem__(self, idx):
        return self.t[idx]

    @property
    def ap(self):
        try:
            return self.t[:]
        except TypeError:
            return self.t.ap()


class K:
    ENGS = ("pe", "ve", "se", "sy", "gp")

    def __init__(self, nc):
        self.nc = nc
        self.eng = {"pe": nc.tensor, "ve": nc.vector, "se": nc.scalar,
                    "sy": nc.sync, "gp": nc.gpsimd}
        self.sem = {}
        self.cnt = {}
        self.hwm = {e: {} for e in self.ENGS}
        for e in self.ENGS:
            self.sem[e] = nc.alloc_semaphore("s_" + e)
            self.cnt[e] = 0
        self.sem["cc"] = nc.alloc_semaphore("s_cc")
        self.cnt["cc"] = 0
        self.bufs = []
        self._uid = 0

    def uname(self, name):
        self._uid += 1
        return f"{name}__{self._uid}"

    def buf(self, t, name=""):
        b = Buf(t, name)
        self.bufs.append(b)
        return b

    def _wait(self, ekey, deps):
        need = {}
        for (sk, v) in deps:
            if sk == "pe" and ekey == "pe":
                continue
            if v > self.hwm[ekey].get(sk, 0):
                need[sk] = max(need.get(sk, 0), v)
        for sk, v in need.items():
            self.eng[ekey].wait_ge(self.sem[sk], v)
            self.hwm[ekey][sk] = v

    def _deps(self, reads, writes):
        deps = []
        for b in reads:
            if b.w:
                deps.append(b.w)
        for b in writes:
            if b.w:
                deps.append(b.w)
            for sk, v in b.rs.items():
                deps.append((sk, v))
        return deps

    def _finish(self, ins, ekey, step, reads, writes):
        self.cnt[ekey] += step
        ins.then_inc(self.sem[ekey], step)
        tok = (ekey, self.cnt[ekey])
        for b in writes:
            b.w = tok
            b.rs = {}
        for b in reads:
            if b.rs.get(ekey, 0) < tok[1]:
                b.rs[ekey] = tok[1]
        return tok

    def op(self, ekey, fn, reads=(), writes=()):
        self._wait(ekey, self._deps(reads, writes))
        ins = fn(self.eng[ekey])
        return self._finish(ins, ekey, 1, reads, writes)

    def dma(self, qkey, out, in_, reads=(), writes=(), **kw):
        self._wait(qkey, self._deps(reads, writes))
        ins = self.eng[qkey].dma_start(out=out, in_=in_, **kw)
        return self._finish(ins, qkey, 16, reads, writes)

    def collective(self, kind, op, groups, in_b, out_b):
        self._wait("gp", self._deps([in_b], [out_b]))
        ins = self.eng["gp"].collective_compute(
            kind, op, replica_groups=groups,
            ins=[in_b.ap.opt()], outs=[out_b.ap.opt()])
        self.cnt["cc"] += 1
        ins.then_inc(self.sem["cc"], 1)
        tok = ("cc", self.cnt["cc"])
        out_b.w = tok
        out_b.rs = {}
        in_b.rs["cc"] = max(in_b.rs.get("cc", 0), tok[1])
        return tok

    def switch4(self, bodyfn):
        self.barrier()
        nc = self.nc
        pid = nc.partition_id()
        engs = [self.eng[e] for e in self.ENGS]
        idx = [eng.compute_val(pid % 4) for eng in engs]
        old_sem, old_cnt, old_hwm = self.sem, dict(self.cnt), self.hwm
        xd = nc.alloc_semaphore("s_xd")
        for i in nc.Switch(engines=engs, index=idx, n=4):
            self.sem = dict(old_sem)
            for e in self.ENGS:
                self.sem[e] = nc.alloc_semaphore(f"s_x{i}_{e}")
            self.cnt = {kk: 0 for kk in self.cnt}
            self.hwm = {e: {} for e in self.ENGS}
            for b in self.bufs:
                b.w = None
                b.rs = {}
            bodyfn(i)
            for e in self.ENGS:
                for sk in self.ENGS:
                    v = self.cnt[sk]
                    if v > self.hwm[e].get(sk, 0):
                        self.eng[e].wait_ge(self.sem[sk], v)
                        self.hwm[e][sk] = v
                self.eng[e].sem_inc(xd, 1)
        self.sem, self.cnt, self.hwm = old_sem, old_cnt, old_hwm
        for e in self.ENGS:
            self.eng[e].wait_ge(xd, 5)
        for b in self.bufs:
            b.w = None
            b.rs = {}

    def barrier(self):
        for e in self.ENGS:
            for sk in list(self.cnt):
                v = self.cnt[sk]
                if v > self.hwm[e].get(sk, 0):
                    self.eng[e].wait_ge(self.sem[sk], v)
                    self.hwm[e][sk] = v
        for b in self.bufs:
            b.w = None
            b.rs = {}

    def finish(self):
        for sk in list(self.cnt):
            v = self.cnt[sk]
            if v > self.hwm["sy"].get(sk, 0):
                self.eng["sy"].wait_ge(self.sem[sk], v)
                self.hwm["sy"][sk] = v


class Prog:
    pass


def _mk_consts(nc):
    c = Prog()
    c.ident_bf = nc.inline_tensor(np.eye(128, dtype=BF), name="ident_bf")
    c.ident_f = nc.inline_tensor(np.eye(128, dtype=np.float32), name="ident_f")
    ones_row = np.ones((1, 128), dtype=BF)
    c.ones_bf = nc.inline_tensor(ones_row, name="ones_bf")
    c.ones_f = nc.inline_tensor(np.ones((1, 128), dtype=np.float32), name="ones_f")
    c.ones_col_bf = nc.inline_tensor(np.ones((128, 1), dtype=BF), name="ones_col_bf")
    c.ones_col_f = nc.inline_tensor(np.ones((128, 1), dtype=np.float32), name="ones_col_f")
    c.one11_f = nc.inline_tensor(np.ones((1, 1), dtype=np.float32), name="one11_f")
    R = np.zeros((128, 128), dtype=np.float32)
    for j in range(64):
        R[2 * j + 1, 2 * j] = -1.0
        R[2 * j, 2 * j + 1] = 1.0
    c.rot_f = nc.inline_tensor(R, name="rot_f")
    tri = np.zeros((128, 4 * 512), dtype=np.float32)
    for di in range(4):
        kk = np.arange(128)[:, None]
        qq = np.arange(512)[None, :]
        tri[:, di * 512:(di + 1) * 512] = np.where(di * 128 + kk > qq, BIGM, 0.0)
    c.tri_f = nc.inline_tensor(tri, name="tri_f")
    return c


def emit_ln_norm(k, st, XTb, SQb, STATb, epsc):
    W = XTb.t.shape[1]
    inv = 1.0 / W
    k.op("ve", lambda e: e.reduce_sum(out=STATb[:, 0:1], in_=XTb[:], axis=AX.X),
         reads=[XTb], writes=[STATb])
    k.op("se", lambda e: e.activation(out=SQb[:], in_=XTb[:], func=AF.Square,
                                      accum_out=STATb[:, 1:2]),
         reads=[XTb], writes=[SQb, STATb])
    k.op("ve", lambda e: e.tensor_scalar(out=STATb[:, 2:3], in0=STATb[:, 0:1],
                                         scalar1=inv, scalar2=None, op0=ALU.mult),
         reads=[STATb], writes=[STATb])
    k.op("ve", lambda e: e.tensor_scalar(out=STATb[:, 3:4], in0=STATb[:, 1:2],
                                         scalar1=inv, scalar2=None, op0=ALU.mult),
         reads=[STATb], writes=[STATb])
    k.op("ve", lambda e: e.tensor_tensor(out=STATb[:, 4:5], in0=STATb[:, 2:3],
                                         in1=STATb[:, 2:3], op=ALU.mult),
         reads=[STATb], writes=[STATb])
    k.op("ve", lambda e: e.tensor_tensor(out=STATb[:, 5:6], in0=STATb[:, 3:4],
                                         in1=STATb[:, 4:5], op=ALU.subtract),
         reads=[STATb], writes=[STATb])
    k.op("se", lambda e: e.activation(out=STATb[:, 5:6], in_=STATb[:, 5:6], func=AF.Ln,
                                      bias=epsc[:, 0:1], scale=1.0),
         reads=[STATb, epsc], writes=[STATb])
    k.op("se", lambda e: e.activation(out=STATb[:, 6:7], in_=STATb[:, 5:6], func=AF.Exp,
                                      bias=0.0, scale=-0.5),
         reads=[STATb], writes=[STATb])
    k.op("ve", lambda e: e.scalar_tensor_tensor(out=STATb[:, 7:8], in0=STATb[:, 2:3],
                                                scalar=-1.0, in1=STATb[:, 6:7],
                                                op0=ALU.mult, op1=ALU.mult),
         reads=[STATb], writes=[STATb])


def emit_xnT(k, st, xsrc, XN_T, tmp, epsc, consts_sb):
    XT, SQ, STAT, XN, PTR = tmp["XT"], tmp["SQ"], tmp["STAT"], tmp["XN"], tmp["PTR"]
    IDB = consts_sb["ident_bf"]
    for t in range(TT):
        b = t % 2
        k.dma("sy", XT[b][:], xsrc.t.ap()[t * 128:(t + 1) * 128, :],
              reads=[xsrc], writes=[XT[b]])
        emit_ln_norm(k, st, XT[b], SQ, STAT[b], epsc)
        k.op("ve", lambda e, b=b: e.tensor_scalar(
            out=XN[b][:], in0=XT[b][:], scalar1=STAT[b][:, 6:7], scalar2=STAT[b][:, 7:8],
            op0=ALU.mult, op1=ALU.add), reads=[XT[b], STAT[b]], writes=[XN[b]])
        for c in range(16):
            pb = c % 2
            k.op("pe", lambda e, b=b, c=c, pb=pb: e.transpose(
                PTR[pb][:], XN[b][:, c * 128:(c + 1) * 128], IDB[:]),
                reads=[XN[b], IDB], writes=[PTR[pb]])
            k.op("ve", lambda e, t=t, c=c, pb=pb: e.tensor_copy(
                out=XN_T[:, c * 2048 + t * 128:c * 2048 + (t + 1) * 128], in_=PTR[pb][:]),
                reads=[PTR[pb]], writes=[XN_T])


def load_bias_cols(k, row_f32_dram, BCOL, consts_sb, pcol, tmprow):
    ONE11 = consts_sb["one11_f"]
    k.dma("sy", tmprow[:], row_f32_dram.t.ap(), reads=[row_f32_dram], writes=[tmprow])
    for c in range(16):
        k.op("pe", lambda e, c=c: e.matmul(
            pcol[:, c:c + 1], tmprow[0:1, c * 128:(c + 1) * 128], ONE11[:],
            start=(c == 0), stop=(c == 15)), reads=[tmprow, ONE11], writes=[pcol])
    k.op("ve", lambda e: e.tensor_copy(out=BCOL[:], in_=pcol[:, 0:16]),
         reads=[pcol], writes=[BCOL])


def build_program(core_variant_mode="switch", stop_after=None, debug_copies=()):
    nc = bass.Bass("TRN2", target_bir_lowering=False)
    cst = _mk_consts(nc)
    k = K(nc)
    st = Prog()

    dt_in = {}

    def ein(name, shape, dt=F32):
        t = nc.dram_tensor(name, list(shape), dt, kind="ExternalInput")
        b = k.buf(t, name)
        dt_in[name] = b
        return b

    x_in = ein("x_in", [TOK, D])
    W0 = {w: ein("L0_" + w, [D, D]) for w in ("Wq", "Wk", "Wv", "Wo")}
    B0 = {w: ein("L0_" + w, [1, D]) for w in ("bq", "bk", "bv", "bo")}
    w1 = ein("L0_w1", [FF, D])
    b1 = ein("L0_b1", [1, FF])
    w2 = ein("L0_w2", [D, FF])
    b2 = ein("L0_b2", [1, D])
    W1 = {w: ein("L1_" + w, [D, D]) for w in ("Wq", "Wk", "Wv", "Wo")}
    lamv = ein("L1_lamv", [1, 512])
    g1 = ein("L1_g", [1, 256])
    wupq = ein("L1_wupq", [FF, D], I32)
    wupsc = ein("L1_wupsc", [FF, 32])
    bup = ein("L1_bup", [1, FF])
    lau = ein("L1_lau", [D, 16])
    lbu = ein("L1_lbu", [16, FF])
    wdnq = ein("L1_wdnq", [D, FF], I32)
    wdnsc = ein("L1_wdnsc", [D, 128])
    bdn = ein("L1_bdn", [1, D])
    lad = ein("L1_lad", [FF, 16])
    lbd = ein("L1_lbd", [16, D])
    rcos = ein("rope_cos", [128, TOK])
    rsin = ein("rope_sin", [128, TOK])

    x_out = k.buf(nc.dram_tensor("x_out", [TOK, D], F32, kind="ExternalOutput"), "x_out")

    kv_in = k.buf(nc.dram_tensor("kv_in", [2 * TOK, D], BF16, kind="Internal"), "kv_in")
    kv_g = k.buf(nc.dram_tensor("kv_g", [4 * 2 * TOK, D], BF16, kind="Internal"), "kv_g")
    q_dram = k.buf(nc.dram_tensor("q_dram", [D, TOK], BF16, kind="Internal"), "q_dram")
    wupT = k.buf(nc.dram_tensor("wupT", [16 * 64 * 128 * 128], BF16, kind="Internal"), "wupT")
    wdnT = k.buf(nc.dram_tensor("wdnT", [64 * 16 * 128 * 128], BF16, kind="Internal"), "wdnT")
    x2t = k.buf(nc.dram_tensor("x2t", [16 * 64 * 128 * 128], BF16, kind="Internal"), "x2t")
    y2_dram = k.buf(nc.dram_tensor("y2_dram", [TOK, D], F32, kind="Internal"), "y2_dram")

    if DBG.get("b1"):
        for nm, shp, dt in (("d_sub", [128, 512], F32), ("d_ptb", [128, 1024], BF16),
                            ("d_msb", [128, 512], F32), ("d_nm", [128, 8], F32),
                            ("d_srow", [1, 1024], F32), ("d_od", [128, 256], F32),
                            ("d_kh", [128, 512], BF16), ("d_qtg", [128, 512], BF16),
                            ("d_vh", [128, 1024], BF16)):
            DBG[nm] = k.buf(nc.dram_tensor(nm, shp, dt, kind="ExternalOutput"), nm)
    dbg_bufs = {}
    for nm in debug_copies:
        src = {"kv_in": kv_in, "kv_g": kv_g, "q_dram": q_dram, "wupT": wupT,
               "wdnT": wdnT, "x2t": x2t, "y2": y2_dram, "x_out": x_out}[nm]
        dbg_bufs[nm] = (src, k.buf(nc.dram_tensor(
            "dbg_" + nm, list(src.t.shape), src.t.dtype, kind="ExternalOutput"), "dbg_" + nm))

    import contextlib
    prog_stack = contextlib.ExitStack()

    def sb(name, shape, dt=F32):
        t = prog_stack.enter_context(nc.sbuf_tensor(name, list(shape), dt))
        return k.buf(t, name)

    consts_sb = {}
    for nm, it, cdt in (("ident_bf", cst.ident_bf, BF16), ("ident_f", cst.ident_f, F32),
                        ("ones_bf", cst.ones_bf, BF16), ("ones_f", cst.ones_f, F32),
                        ("ones_col_bf", cst.ones_col_bf, BF16),
                        ("ones_col_f", cst.ones_col_f, F32),
                        ("one11_f", cst.one11_f, F32), ("rot_f", cst.rot_f, F32)):
        b = sb(nm + "_sb", list(it.shape), cdt)
        k.dma("sy", b[:], it.ap(), writes=[b])
        consts_sb[nm] = b
    epsc = sb("epsc", [128, 1])
    k.op("ve", lambda e: e.memset(epsc[:], EPS), writes=[epsc])
    qscale = sb("qscale", [128, 1])
    k.op("ve", lambda e: e.memset(qscale[:], INV_SQRT_HD), writes=[qscale])
    ln1mli = sb("ln1mli", [128, 1])
    k.op("ve", lambda e: e.memset(ln1mli[:], math.log(1.0 - LAMBDA_INIT)), writes=[ln1mli])

    brow_bf = {}
    brow_f32 = sb("brow_f32_stage", [1, D])
    _brow_f32 = brow_f32
    for nm, bsrc in (("bv0", B0["bv"]), ("bo0", B0["bo"]), ("b2", b2), ("bdn", bdn)):
        b = sb("brow_" + nm, [1, bsrc.t.shape[1]], BF16)
        k.dma("sy", _brow_f32[:], bsrc.t.ap(), reads=[bsrc], writes=[_brow_f32])
        k.op("se", lambda e, b=b: e.copy(out=b[:], in_=_brow_f32[:]),
             reads=[_brow_f32], writes=[b])
        brow_bf[nm] = b

    def phase_A(L):
        ph = contextlib.ExitStack()

        def psb(name, shape, dt=F32):
            return k.buf(ph.enter_context(nc.sbuf_tensor(k.uname(name), list(shape), dt)), name)

        def pps(name, shape, dt=F32):
            return k.buf(ph.enter_context(nc.psum_tensor(k.uname(name), list(shape), dt)), name)

        XN_T = psb("XN_T", [128, 16 * 2048], BF16)
        tmp = {
            "XT": [psb("A_XT0", [128, 2048]), psb("A_XT1", [128, 2048])],
            "SQ": psb("A_SQ", [128, 2048]),
            "STAT": [psb("A_ST0", [128, 8]), psb("A_ST1", [128, 8])],
            "XN": [psb("A_XN0", [128, 2048], BF16), psb("A_XN1", [128, 2048], BF16)],
            "PTR": [pps("A_PT0", [128, 128], BF16), pps("A_PT1", [128, 128], BF16)],
        }
        xsrc = x_in if L == 0 else x_out
        emit_xnT(k, st, xsrc, XN_T, tmp, epsc, consts_sb)

        WG = [psb("A_WG0", [128, 16 * 512], BF16), psb("A_WG1", [128, 16 * 512], BF16)]
        psp = [pps("A_PP0", [128, 512]), pps("A_PP1", [128, 512])]
        psr = [pps("A_PR0", [128, 512]), pps("A_PR1", [128, 512])]
        QP = [psb("A_QP0", [128, 512]), psb("A_QP1", [128, 512])]
        T1 = psb("A_T1", [128, 512])
        KSTG = [psb("A_KS0", [128, 512], BF16), psb("A_KS1", [128, 512], BF16)]
        BCOL = {nm: psb("A_BC_" + nm, [128, 16]) for nm in ("bq", "bk")}
        BROW = psb("A_BROW", [1, 2048])
        pcol = pps("A_PCOL", [128, 16])
        if L == 0:
            RC = psb("A_RC", [128, TOK])
            RS = psb("A_RS", [128, TOK])
            k.dma("sy", RC[:], rcos.t.ap(), reads=[rcos], writes=[RC])
            k.dma("sy", RS[:], rsin.t.ap(), reads=[rsin], writes=[RS])
        else:
            RC = RS = None

        Wm = W0 if L == 0 else W1
        if L == 0:
            for nm in ("bq", "bk"):
                load_bias_cols(k, B0[nm], BCOL[nm], consts_sb, pcol, BROW)

        def proj(wname, out_kind, bcol=None, brow=None, scale_ap=None, do_rope=False):
            W = Wm[wname]
            wap = W.t.ap().rearrange("(c p) n -> p c n", p=128)
            for g in range(4):
                wb = g % 2
                k.dma("gp", WG[wb][:].rearrange("p (c n) -> p c n", n=512),
                      wap[:, :, g * 512:(g + 1) * 512], reads=[W], writes=[WG[wb]])
                if out_kind in ("K", "Q"):
                    for tg in range(4):
                        for dcg in range(4):
                            pb = (tg * 4 + dcg) % 2
                            dout = g * 512 + dcg * 128
                            for c in range(16):
                                k.op("pe", lambda e, c=c, wb=wb, dcg=dcg, tg=tg, pb=pb: e.matmul(
                                    psp[pb][:],
                                    WG[wb][:, c * 512 + dcg * 128:c * 512 + (dcg + 1) * 128],
                                    XN_T[:, c * 2048 + tg * 512:c * 2048 + (tg + 1) * 512],
                                    start=(c == 0), stop=(c == 15)),
                                    reads=[WG[wb], XN_T], writes=[psp[pb]])
                            dc16 = dout // 128
                            if do_rope:
                                if bcol is not None and scale_ap is not None:
                                    k.op("ve", lambda e, pb=pb, dc16=dc16: e.tensor_scalar(
                                        out=QP[pb][:], in0=psp[pb][:],
                                        scalar1=bcol[:, dc16:dc16 + 1],
                                        scalar2=scale_ap[:, 0:1], op0=ALU.add, op1=ALU.mult),
                                        reads=[psp[pb], bcol, scale_ap], writes=[QP[pb]])
                                elif bcol is not None:
                                    k.op("ve", lambda e, pb=pb, dc16=dc16: e.tensor_scalar(
                                        out=QP[pb][:], in0=psp[pb][:],
                                        scalar1=bcol[:, dc16:dc16 + 1],
                                        scalar2=None, op0=ALU.add),
                                        reads=[psp[pb], bcol], writes=[QP[pb]])
                                else:
                                    k.op("ve", lambda e, pb=pb: e.tensor_copy(
                                        out=QP[pb][:], in_=psp[pb][:]),
                                        reads=[psp[pb]], writes=[QP[pb]])
                                k.op("pe", lambda e, pb=pb: e.matmul(
                                    psr[pb][:], consts_sb["rot_f"][:], QP[pb][:],
                                    start=True, stop=True),
                                    reads=[consts_sb["rot_f"], QP[pb]], writes=[psr[pb]])
                                tgs = slice(tg * 512, (tg + 1) * 512)
                                k.op("ve", lambda e, pb=pb, tgs=tgs: e.tensor_tensor(
                                    out=T1[:], in0=QP[pb][:], in1=RC[:, tgs], op=ALU.mult),
                                    reads=[QP[pb], RC], writes=[T1])
                                k.op("ve", lambda e, pb=pb, tgs=tgs: e.tensor_tensor(
                                    out=QP[pb][:], in0=psr[pb][:], in1=RS[:, tgs], op=ALU.mult),
                                    reads=[psr[pb], RS], writes=[QP[pb]])
                                k.op("ve", lambda e, pb=pb: e.tensor_tensor(
                                    out=KSTG[pb][:], in0=T1[:], in1=QP[pb][:], op=ALU.add),
                                    reads=[T1, QP[pb]], writes=[KSTG[pb]])
                            else:
                                if scale_ap is not None:
                                    k.op("ve", lambda e, pb=pb: e.tensor_scalar(
                                        out=KSTG[pb][:], in0=psp[pb][:], scalar1=scale_ap[:, 0:1],
                                        scalar2=None, op0=ALU.mult),
                                        reads=[psp[pb], scale_ap], writes=[KSTG[pb]])
                                else:
                                    k.op("ve", lambda e, pb=pb: e.tensor_copy(
                                        out=KSTG[pb][:], in_=psp[pb][:]),
                                        reads=[psp[pb]], writes=[KSTG[pb]])
                            dst = kv_in if out_kind == "K" else q_dram
                            k.dma("sy", dst.t.ap()[dout:dout + 128, tg * 512:(tg + 1) * 512],
                                  KSTG[pb][:], reads=[KSTG[pb]], writes=[dst])
                else:
                    for t in range(TT):
                        pb = t % 2
                        last = 15 if brow is None else -1
                        for c in range(16):
                            k.op("pe", lambda e, c=c, wb=wb, t=t, pb=pb, last=last: e.matmul(
                                psp[pb][:], XN_T[:, c * 2048 + t * 128:c * 2048 + (t + 1) * 128],
                                WG[wb][:, c * 512:(c + 1) * 512],
                                start=(c == 0), stop=(c == last)),
                                reads=[WG[wb], XN_T], writes=[psp[pb]])
                        if brow is not None:
                            k.op("pe", lambda e, pb=pb, g=g: e.matmul(
                                psp[pb][:], consts_sb["ones_bf"][:],
                                brow[0:1, g * 512:(g + 1) * 512], start=False, stop=True),
                                reads=[consts_sb["ones_bf"], brow], writes=[psp[pb]])
                        k.op("se", lambda e, pb=pb: e.copy(out=KSTG[pb][:], in_=psp[pb][:]),
                             reads=[psp[pb]], writes=[KSTG[pb]])
                        k.dma("sy", kv_in.t.ap()[2048 + t * 128:2048 + (t + 1) * 128,
                                                 g * 512:(g + 1) * 512],
                              KSTG[pb][:], reads=[KSTG[pb]], writes=[kv_in])

        if L == 0:
            proj("Wk", "K", bcol=BCOL["bk"], do_rope=True)
            proj("Wv", "V", brow=brow_bf["bv0"])
        else:
            proj("Wk", "K")
            proj("Wv", "V")
        groups = [[0, 1, 2, 3], [4, 5, 6, 7]]
        k.collective("AllGather", ALU.bypass, groups, kv_in, kv_g)
        if L == 0:
            proj("Wq", "Q", bcol=BCOL["bq"], scale_ap=qscale, do_rope=True)
        else:
            proj("Wq", "Q", scale_ap=qscale)
        k.barrier()
        ph.close()

    def phase_B0(nqg=NQG, nh=16, upto="full"):
        ph = contextlib.ExitStack()

        def psb(name, shape, dt=F32):
            return k.buf(ph.enter_context(nc.sbuf_tensor(k.uname(name), list(shape), dt)), name)

        def pps(name, shape, dt=F32):
            return k.buf(ph.enter_context(nc.psum_tensor(k.uname(name), list(shape), dt)), name)

        QTG = psb("B_QTG", [128, 16 * 512], BF16)
        KH = [psb("B_KH0", [128, S], BF16)]
        VH = [psb("B_VH0", [128, 64 * 129], BF16)]
        MSB = psb("B_MSB", [128, 512])
        NM = psb("B_NM", [128, 4])
        MX = psb("B_MX", [128, 4])
        NROW = psb("B_NROW", [1, 512])
        SUB = [psb("B_SUB0", [128, 512]), psb("B_SUB1", [128, 512])]
        PTB = [psb("B_PTB0", [128, 512], BF16), psb("B_PTB1", [128, 512], BF16)]
        WOS = psb("B_WOS", [128, 16 * 2048], BF16)
        OTG = psb("B_OTG", [128, 16 * 512], BF16)
        RQ = psb("B_RQ", [128, 1])
        OFIN = psb("B_OFIN", [128, 128], BF16)
        AO = psb("B_AO", [128, 2048])
        XT = psb("B_XT", [128, 2048])
        SQ = psb("B_SQ", [128, 2048])
        STAT = psb("B_STAT", [128, 8])
        Y1 = psb("B_Y1", [128, 2048])
        pst = [pps("B_PST0", [128, 512]), pps("B_PST1", [128, 512])]
        poh = [pps(f"B_POH{i}", [128, 129]) for i in range(4)]
        pwo = [pps("B_PWO0", [128, 512]), pps("B_PWO1", [128, 512])]

        woap = W0["Wo"].t.ap().rearrange("(c p) n -> p c n", p=128)
        for cq in range(4):
            k.dma("gp", WOS[:].rearrange("p (c n) -> p c n", n=2048)[:, cq * 4:(cq + 1) * 4, :],
                  woap[:, cq * 4:(cq + 1) * 4, :], reads=[W0["Wo"]], writes=[WOS])
        k.op("ve", lambda e: e.memset(VH[0][:], 1.0), writes=[VH[0]])

        for qg in range(nqg):
            k.dma("sy", QTG[:].rearrange("p (c n) -> p c n", n=512),
                  q_dram.t.ap().rearrange("(c p) n -> p c n", p=128)[:, :, qg * 512:(qg + 1) * 512],
                  reads=[q_dram], writes=[QTG])
            for h in range(nh):
                hb = 0
                for j in range(NPT):
                    r, loc = OWN[j]
                    k.dma("sy", KH[hb][:, j * 512:(j + 1) * 512],
                          kv_g.t.ap()[r * 4096 + h * 128:r * 4096 + (h + 1) * 128,
                                      loc * 512:(loc + 1) * 512],
                          reads=[kv_g], writes=[KH[hb]])
                for j in range(NPT):
                    r, loc = OWN[j]
                    src = kv_g.t.ap()[r * 4096 + 2048 + loc * 512:
                                      r * 4096 + 2048 + (loc + 1) * 512,
                                      h * 128:(h + 1) * 128]
                    src3 = src.rearrange("(kt p) e -> p kt e", p=128)
                    dst3 = VH[hb][:].rearrange("p (kc w) -> p kc w", w=129)[:, 4 * j:4 * j + 4, 0:128]
                    k.dma("sy", dst3, src3, reads=[kv_g], writes=[VH[hb]])
                if upto == "loads":
                    continue
                for qt in range(4):
                    lhs = (h * 512 + qt * 128, h * 512 + (qt + 1) * 128)
                    for kc in range(16):
                        pb = kc % 2
                        k.op("pe", lambda e, lhs=lhs, kc=kc, pb=pb, hb=hb: e.matmul(
                            pst[pb][:], QTG[:, lhs[0]:lhs[1]],
                            KH[hb][:, kc * 512:(kc + 1) * 512], start=True, stop=True),
                            reads=[QTG, KH[hb]], writes=[pst[pb]])
                        k.op("ve", lambda e, pb=pb: e.tensor_reduce(
                            out=MX[:, 0:1] if False else MX[:, 1:2], in_=pst[pb][:],
                            axis=AX.X, op=ALU.max, negate=True),
                            reads=[pst[pb]], writes=[MX])
                        if kc == 0:
                            k.op("ve", lambda e, qt=qt: e.tensor_copy(
                                out=NM[:, qt:qt + 1], in_=MX[:, 1:2]),
                                reads=[MX], writes=[NM])
                        else:
                            k.op("ve", lambda e, qt=qt: e.tensor_tensor(
                                out=NM[:, qt:qt + 1], in0=NM[:, qt:qt + 1],
                                in1=MX[:, 1:2], op=ALU.min),
                                reads=[MX, NM], writes=[NM])
                if upto == "passA":
                    continue
                for qt in range(4):
                    k.op("pe", lambda e, qt=qt: e.matmul(
                        pst[0][0:1, qt * 128:(qt + 1) * 128], NM[:, qt:qt + 1],
                        consts_sb["ident_f"][:], start=(qt == 0), stop=(qt == 3)),
                        reads=[NM, consts_sb["ident_f"]], writes=[pst[0]])
                k.op("ve", lambda e: e.tensor_copy(out=NROW[:], in_=pst[0][0:1, 0:512]),
                     reads=[pst[0]], writes=[NROW])
                k.op("pe", lambda e: e.matmul(pst[1][:], consts_sb["ones_f"][:], NROW[:],
                                              start=True, stop=True),
                     reads=[consts_sb["ones_f"], NROW], writes=[pst[1]])
                k.op("ve", lambda e: e.tensor_copy(out=MSB[:], in_=pst[1][:]),
                     reads=[pst[1]], writes=[MSB])
                if upto == "msb":
                    continue
                for kc in range(64):
                    pb = kc % 2
                    k.op("pe", lambda e, kc=kc, pb=pb, hb=hb, h=h: e.matmul(
                        pst[pb][:], KH[hb][:, kc * 128:(kc + 1) * 128],
                        QTG[:, h * 512:(h + 1) * 512], start=True, stop=True),
                        reads=[KH[hb], QTG], writes=[pst[pb]])
                    k.op("ve", lambda e, pb=pb: e.tensor_tensor(
                        out=SUB[pb][:], in0=pst[pb][:], in1=MSB[:], op=ALU.subtract),
                        reads=[pst[pb], MSB], writes=[SUB[pb]])
                    k.op("se", lambda e, pb=pb: e.activation(
                        out=PTB[pb][:], in_=SUB[pb][:], func=AF.Exp),
                        reads=[SUB[pb]], writes=[PTB[pb]])
                    for qt in range(4):
                        k.op("pe", lambda e, kc=kc, pb=pb, qt=qt, hb=hb: e.matmul(
                            poh[qt][:], PTB[pb][:, qt * 128:(qt + 1) * 128],
                            VH[hb][:, kc * 129:(kc + 1) * 129],
                            start=(kc == 0), stop=(kc == 63)),
                            reads=[PTB[pb], VH[hb]], writes=[poh[qt]])
                if upto == "passB":
                    continue
                for qt in range(4):
                    k.op("ve", lambda e, qt=qt: e.reciprocal(
                        out=RQ[:], in_=poh[qt][:, 128:129]), reads=[poh[qt]], writes=[RQ])
                    k.op("ve", lambda e, qt=qt: e.tensor_scalar(
                        out=OFIN[:], in0=poh[qt][:, 0:128], scalar1=RQ[:, 0:1],
                        scalar2=None, op0=ALU.mult), reads=[poh[qt], RQ], writes=[OFIN])
                    trg = pst[0][:, 0:64].bitcast(BF16)
                    k.op("pe", lambda e, trg=trg: e.transpose(
                        trg, OFIN[:], consts_sb["ident_bf"][:]),
                        reads=[OFIN, consts_sb["ident_bf"]], writes=[pst[0]])
                    k.op("ve", lambda e, h=h, qt=qt, trg=trg: e.tensor_copy(
                        out=OTG[:, h * 512 + qt * 128:h * 512 + (qt + 1) * 128], in_=trg),
                        reads=[pst[0]], writes=[OTG])
            if upto != "full":
                continue
            for qt in range(4):
                for dg in range(4):
                    wb = dg % 2
                    for c in range(16):
                        k.op("pe", lambda e, c=c, qt=qt, dg=dg, wb=wb: e.matmul(
                            pwo[wb][:], OTG[:, c * 512 + qt * 128:c * 512 + (qt + 1) * 128],
                            WOS[:, c * 2048 + dg * 512:c * 2048 + (dg + 1) * 512],
                            start=(c == 0), stop=False),
                            reads=[OTG, WOS], writes=[pwo[wb]])
                    k.op("pe", lambda e, dg=dg, wb=wb: e.matmul(
                        pwo[wb][:], consts_sb["ones_bf"][:],
                        brow_bf["bo0"][0:1, dg * 512:(dg + 1) * 512], start=False, stop=True),
                        reads=[consts_sb["ones_bf"], brow_bf["bo0"]], writes=[pwo[wb]])
                    k.op("se", lambda e, dg=dg, wb=wb: e.copy(
                        out=AO[:, dg * 512:(dg + 1) * 512], in_=pwo[wb][:]),
                        reads=[pwo[wb]], writes=[AO])
                row0 = qg * 512 + qt * 128
                k.dma("sy", XT[:], x_in.t.ap()[row0:row0 + 128, :],
                      reads=[x_in], writes=[XT])
                emit_ln_norm(k, st, AO, SQ, STAT, epsc)
                k.op("ve", lambda e: e.tensor_scalar(
                    out=Y1[:], in0=AO[:], scalar1=STAT[:, 6:7], scalar2=STAT[:, 7:8],
                    op0=ALU.mult, op1=ALU.add), reads=[AO, STAT], writes=[Y1])
                k.op("ve", lambda e: e.tensor_tensor(
                    out=XT[:], in0=XT[:], in1=Y1[:], op=ALU.add),
                    reads=[Y1], writes=[XT])
                k.dma("sy", x_out.t.ap()[row0:row0 + 128, :], XT[:],
                      reads=[XT], writes=[x_out])
        k.barrier()
        ph.close()

    def phase_B1():
        ph = contextlib.ExitStack()

        def psb(name, shape, dt=F32):
            return k.buf(ph.enter_context(nc.sbuf_tensor(k.uname(name), list(shape), dt)), name)

        def pps(name, shape, dt=F32):
            return k.buf(ph.enter_context(nc.psum_tensor(k.uname(name), list(shape), dt)), name)

        QTG = psb("B1_QTG", [128, 16 * 512], BF16)
        KH = psb("B1_KH", [128, 2 * S], BF16)
        VH = psb("B1_VH", [128, 64 * 256], BF16)
        MSB = [psb("B1_MSB0", [128, 512]), psb("B1_MSB1", [128, 512])]
        NM = psb("B1_NM", [128, 8])
        MX = psb("B1_MX", [128, 1])
        NROW = [psb("B1_NR0", [1, 512]), psb("B1_NR1", [1, 512])]
        TRIS = psb("B1_TRIS", [128, 4 * 512])
        SUB = psb("B1_SUB", [128, 512])
        PTB = [psb("B1_PTB0", [128, 2 * 512], BF16), psb("B1_PTB1", [128, 2 * 512], BF16)]
        WOG = psb("B1_WOG", [128, 16 * 512], BF16)
        GCOL = psb("B1_GCOL", [128, 2])
        OTG = psb("B1_OTG", [128, 16 * 512], BF16)
        SROW = psb("B1_SROW", [1, 2 * 512])
        RC1 = psb("B1_RC1", [128, 1])
        RC2 = psb("B1_RC2", [128, 1])
        OD1 = psb("B1_OD1", [128, 256])
        OD = psb("B1_OD", [128, 256])
        SQ2 = psb("B1_SQ2", [128, 256])
        RST = psb("B1_RST", [128, 3])
        OFIN = psb("B1_OFIN", [128, 256], BF16)
        AOS = [psb("B1_AOS0", [128, 512]), psb("B1_AOS1", [128, 512])]
        LAMV = psb("B1_LAMV", [1, 512])
        TLAM = psb("B1_TLAM", [1, 128])
        E12 = psb("B1_E12", [1, 4])
        LIC = psb("B1_LIC", [1, 1])
        NEGLAM = psb("B1_NEGLAM", [128, 1])
        pst = [pps("B1_PST0", [128, 512]), pps("B1_PST1", [128, 512])]
        psO = [pps(f"B1_PO{i}", [128, 512]) for i in range(4)]
        psS = [pps("B1_PS0", [1, 512]), pps("B1_PS1", [1, 512])]

        k.dma("sy", TRIS[:], cst.tri_f.ap(), writes=[TRIS])
        k.dma("sy", LAMV[:], lamv.t.ap(), reads=[lamv], writes=[LAMV])
        for i in range(2):
            k.op("ve", lambda e, i=i: e.tensor_tensor(
                out=TLAM[:], in0=LAMV[0:1, i * 256:i * 256 + 128],
                in1=LAMV[0:1, i * 256 + 128:i * 256 + 256], op=ALU.mult),
                reads=[LAMV], writes=[TLAM])
            k.op("ve", lambda e, i=i: e.reduce_sum(
                out=E12[0:1, i:i + 1], in_=TLAM[:], axis=AX.X),
                reads=[TLAM], writes=[E12])
            k.op("se", lambda e, i=i: e.activation(
                out=E12[0:1, i:i + 1], in_=E12[0:1, i:i + 1], func=AF.Exp),
                reads=[E12], writes=[E12])
        k.op("ve", lambda e: e.memset(LIC[:], LAMBDA_INIT), writes=[LIC])
        k.op("ve", lambda e: e.tensor_tensor(
            out=E12[0:1, 2:3], in0=E12[0:1, 1:2], in1=E12[0:1, 0:1], op=ALU.subtract),
            reads=[E12], writes=[E12])
        k.op("ve", lambda e: e.tensor_tensor(
            out=E12[0:1, 3:4], in0=E12[0:1, 2:3], in1=LIC[:], op=ALU.subtract),
            reads=[E12, LIC], writes=[E12])
        k.op("pe", lambda e: e.matmul(pst[0][:, 0:1], consts_sb["ones_f"][:],
                                      E12[0:1, 3:4], start=True, stop=True),
             reads=[consts_sb["ones_f"], E12], writes=[pst[0]])
        k.op("ve", lambda e: e.tensor_copy(out=NEGLAM[:], in_=pst[0][:, 0:1]),
             reads=[pst[0]], writes=[NEGLAM])
        k.dma("sy", SROW[0:1, 0:256], g1.t.ap(), reads=[g1], writes=[SROW])
        for i in range(2):
            k.op("pe", lambda e, i=i: e.matmul(
                pst[0][:, 4 + i:5 + i], SROW[0:1, i * 128:(i + 1) * 128],
                consts_sb["one11_f"][:], start=(i == 0), stop=(i == 1)),
                reads=[SROW, consts_sb["one11_f"]], writes=[pst[0]])
        k.op("ve", lambda e: e.tensor_copy(out=GCOL[:], in_=pst[0][:, 4:6]),
             reads=[pst[0]], writes=[GCOL])

        def body(var):
            zs = zeb(var)
            for qg in range(NQG):
                z = zs[qg]
                nk5 = z + 1
                nk = 4 * z + 4
                k.dma("sy", QTG[:].rearrange("p (c n) -> p c n", n=512),
                      q_dram.t.ap().rearrange("(c p) n -> p c n", p=128)[
                          :, :, qg * 512:(qg + 1) * 512],
                      reads=[q_dram], writes=[QTG])
                for h in range(8):
                    for cc in range(2):
                        for j in range(nk5):
                            r, loc = OWN[j]
                            k.dma("sy", KH[:, cc * S + j * 512:cc * S + (j + 1) * 512],
                                  kv_g.t.ap()[r * 4096 + h * 256 + cc * 128:
                                              r * 4096 + h * 256 + (cc + 1) * 128,
                                              loc * 512:(loc + 1) * 512],
                                  reads=[kv_g], writes=[KH])
                    for j in range(nk5):
                        r, loc = OWN[j]
                        src = kv_g.t.ap()[r * 4096 + 2048 + loc * 512:
                                          r * 4096 + 2048 + (loc + 1) * 512,
                                          h * 256:(h + 1) * 256]
                        src3 = src.rearrange("(kt p) e -> p kt e", p=128)
                        dst3 = VH[:].rearrange("p (kc w) -> p kc w", w=256)[
                            :, 4 * j:4 * j + 4, :]
                        k.dma("sy", dst3, src3, reads=[kv_g], writes=[VH])
                    for cc in range(2):
                        for qt in range(4):
                            l0 = (2 * h + cc) * 512 + qt * 128
                            for kc in range(nk5):
                                pb = kc % 2
                                k.op("pe", lambda e, l0=l0, kc=kc, pb=pb, cc=cc: e.matmul(
                                    pst[pb][:], QTG[:, l0:l0 + 128],
                                    KH[:, cc * S + kc * 512:cc * S + (kc + 1) * 512],
                                    start=True, stop=True),
                                    reads=[QTG, KH], writes=[pst[pb]])
                                k.op("ve", lambda e, pb=pb: e.tensor_reduce(
                                    out=MX[:], in_=pst[pb][:], axis=AX.X,
                                    op=ALU.max, negate=True),
                                    reads=[pst[pb]], writes=[MX])
                                col = cc * 4 + qt
                                if kc == 0:
                                    k.op("ve", lambda e, col=col: e.tensor_copy(
                                        out=NM[:, col:col + 1], in_=MX[:]),
                                        reads=[MX], writes=[NM])
                                else:
                                    k.op("ve", lambda e, col=col: e.tensor_tensor(
                                        out=NM[:, col:col + 1], in0=NM[:, col:col + 1],
                                        in1=MX[:], op=ALU.min),
                                        reads=[MX], writes=[NM])
                    for cc in range(2):
                        for qt in range(4):
                            k.op("pe", lambda e, cc=cc, qt=qt: e.matmul(
                                pst[0][0:1, qt * 128:(qt + 1) * 128],
                                NM[:, cc * 4 + qt:cc * 4 + qt + 1],
                                consts_sb["ident_f"][:], start=(qt == 0), stop=(qt == 3)),
                                reads=[NM, consts_sb["ident_f"]], writes=[pst[0]])
                        k.op("ve", lambda e, cc=cc: e.tensor_copy(
                            out=NROW[cc][:], in_=pst[0][0:1, 0:512]),
                            reads=[pst[0]], writes=[NROW[cc]])
                        k.op("pe", lambda e, cc=cc: e.matmul(
                            pst[1][:], consts_sb["ones_f"][:], NROW[cc][:],
                            start=True, stop=True),
                            reads=[consts_sb["ones_f"], NROW[cc]], writes=[pst[1]])
                        k.op("ve", lambda e, cc=cc: e.tensor_copy(
                            out=MSB[cc][:], in_=pst[1][:]),
                            reads=[pst[1]], writes=[MSB[cc]])
                    for kc in range(nk):
                        pb = kc % 2
                        di = kc - 4 * z
                        for cc in range(2):
                            k.op("pe", lambda e, kc=kc, pb=pb, cc=cc, h=h: e.matmul(
                                pst[pb][:], KH[:, cc * S + kc * 128:cc * S + (kc + 1) * 128],
                                QTG[:, (2 * h + cc) * 512:(2 * h + cc + 1) * 512],
                                start=True, stop=True),
                                reads=[KH, QTG], writes=[pst[pb]])
                            k.op("ve", lambda e, pb=pb, cc=cc: e.tensor_tensor(
                                out=SUB[:], in0=pst[pb][:], in1=MSB[cc][:],
                                op=ALU.subtract),
                                reads=[pst[pb], MSB[cc]], writes=[SUB])
                            if di >= 0:
                                k.op("ve", lambda e, di=di: e.tensor_tensor(
                                    out=SUB[:], in0=SUB[:],
                                    in1=TRIS[:, di * 512:(di + 1) * 512],
                                    op=ALU.subtract),
                                    reads=[TRIS], writes=[SUB])
                            k.op("se", lambda e, pb=pb, cc=cc: e.activation(
                                out=PTB[pb][:, cc * 512:(cc + 1) * 512], in_=SUB[:],
                                func=AF.Exp),
                                reads=[SUB], writes=[PTB[pb]])
                            if (DBG.get("b1") and var == 0 and qg == 0 and h == 0
                                    and kc == 0 and cc == 0):
                                k.dma("sy", DBG["d_sub"].t.ap(), SUB[:],
                                      reads=[SUB], writes=[DBG["d_sub"]])
                                k.dma("sy", DBG["d_ptb"].t.ap(), PTB[pb][:],
                                      reads=[PTB[pb]], writes=[DBG["d_ptb"]])
                            k.op("pe", lambda e, pb=pb, cc=cc, kc=kc, nk=nk: e.matmul(
                                psS[cc][:], consts_sb["ones_col_bf"][:],
                                PTB[pb][:, cc * 512:(cc + 1) * 512],
                                start=(kc == 0), stop=(kc == nk - 1)),
                                reads=[consts_sb["ones_col_bf"], PTB[pb]], writes=[psS[cc]])
                            for qt in range(4):
                                k.op("pe", lambda e, pb=pb, cc=cc, qt=qt, kc=kc, nk=nk: e.matmul(
                                    psO[cc * 2 + qt // 2][:, (qt % 2) * 256:(qt % 2 + 1) * 256],
                                    PTB[pb][:, cc * 512 + qt * 128:cc * 512 + (qt + 1) * 128],
                                    VH[:, kc * 256:(kc + 1) * 256],
                                    start=(kc == 0 and qt % 2 == 0),
                                    stop=(kc == nk - 1)),
                                    reads=[PTB[pb], VH], writes=[psO[cc * 2 + qt // 2]])
                    if DBG.get("b1") and var == 0 and qg == 0 and h == 0:
                        k.dma("sy", DBG["d_msb"].t.ap(), MSB[0][:],
                              reads=[MSB[0]], writes=[DBG["d_msb"]])
                        k.dma("sy", DBG["d_nm"].t.ap(), NM[:], reads=[NM], writes=[DBG["d_nm"]])
                        k.dma("sy", DBG["d_kh"].t.ap(), KH[:, 0:512], reads=[KH],
                              writes=[DBG["d_kh"]])
                        k.dma("sy", DBG["d_qtg"].t.ap(), QTG[:, 0:512], reads=[QTG],
                              writes=[DBG["d_qtg"]])
                        k.dma("sy", DBG["d_vh"].t.ap(), VH[:, 0:1024], reads=[VH],
                              writes=[DBG["d_vh"]])
                    for cc in range(2):
                        k.op("ve", lambda e, cc=cc: e.tensor_copy(
                            out=SROW[0:1, cc * 512:(cc + 1) * 512], in_=psS[cc][:]),
                            reads=[psS[cc]], writes=[SROW])
                    for qt in range(4):
                        for cc in range(2):
                            k.op("pe", lambda e, cc=cc, qt=qt: e.matmul(
                                pst[0][:, 8 + cc:9 + cc],
                                SROW[0:1, cc * 512 + qt * 128:cc * 512 + (qt + 1) * 128],
                                consts_sb["one11_f"][:], start=(cc == 0), stop=(cc == 1)),
                                reads=[SROW, consts_sb["one11_f"]], writes=[pst[0]])
                        k.op("ve", lambda e: e.reciprocal(out=RC1[:], in_=pst[0][:, 8:9]),
                             reads=[pst[0]], writes=[RC1])
                        k.op("ve", lambda e: e.reciprocal(out=RC2[:], in_=pst[0][:, 9:10]),
                             reads=[pst[0]], writes=[RC2])
                        k.op("ve", lambda e: e.tensor_tensor(
                            out=RC2[:], in0=RC2[:], in1=NEGLAM[:], op=ALU.mult),
                            reads=[NEGLAM], writes=[RC2])
                        k.op("ve", lambda e, qt=qt: e.tensor_scalar(
                            out=OD1[:], in0=psO[qt // 2][:, (qt % 2) * 256:(qt % 2 + 1) * 256],
                            scalar1=RC1[:, 0:1], scalar2=None, op0=ALU.mult),
                            reads=[psO[qt // 2], RC1], writes=[OD1])
                        k.op("ve", lambda e, qt=qt: e.scalar_tensor_tensor(
                            out=OD[:], in0=psO[2 + qt // 2][:, (qt % 2) * 256:(qt % 2 + 1) * 256],
                            scalar=RC2[:, 0:1], in1=OD1[:], op0=ALU.mult, op1=ALU.add),
                            reads=[psO[2 + qt // 2], RC2, OD1], writes=[OD])
                        if (DBG.get("b1") and var == 0 and qg == 0 and h == 0 and qt == 0):
                            k.dma("sy", DBG["d_od"].t.ap(), OD[:], reads=[OD],
                                  writes=[DBG["d_od"]])
                            k.dma("sy", DBG["d_srow"].t.ap(), SROW[:], reads=[SROW],
                                  writes=[DBG["d_srow"]])
                        k.op("se", lambda e: e.activation(
                            out=SQ2[:], in_=OD[:], func=AF.Square,
                            accum_out=RST[:, 0:1]),
                            reads=[OD], writes=[SQ2, RST])
                        k.op("ve", lambda e: e.tensor_scalar(
                            out=RST[:, 1:2], in0=RST[:, 0:1], scalar1=1.0 / 256,
                            scalar2=epsc[:, 0:1], op0=ALU.mult, op1=ALU.add),
                            reads=[RST, epsc], writes=[RST])
                        k.op("se", lambda e: e.activation(
                            out=RST[:, 1:2], in_=RST[:, 1:2], func=AF.Ln),
                            reads=[RST], writes=[RST])
                        k.op("se", lambda e: e.activation(
                            out=RST[:, 2:3], in_=RST[:, 1:2], func=AF.Exp,
                            bias=ln1mli[:, 0:1], scale=-0.5),
                            reads=[RST, ln1mli], writes=[RST])
                        k.op("ve", lambda e: e.tensor_scalar(
                            out=OFIN[:], in0=OD[:], scalar1=RST[:, 2:3], scalar2=None,
                            op0=ALU.mult), reads=[OD, RST], writes=[OFIN])
                        for half in range(2):
                            trg = pst[1][:, 0:64].bitcast(BF16)
                            k.op("pe", lambda e, half=half, trg=trg: e.transpose(
                                trg, OFIN[:, half * 128:(half + 1) * 128],
                                consts_sb["ident_bf"][:]),
                                reads=[OFIN, consts_sb["ident_bf"]], writes=[pst[1]])
                            k.op("ve", lambda e, h=h, half=half, qt=qt, trg=trg: e.tensor_copy(
                                out=OTG[:, (2 * h + half) * 512 + qt * 128:
                                        (2 * h + half) * 512 + (qt + 1) * 128], in_=trg),
                                reads=[pst[1]], writes=[OTG])
                for dg in range(4):
                    wosrc = W1["Wo"].t.ap().rearrange("(c p) n -> p c n", p=128)[
                        :, :, dg * 512:(dg + 1) * 512]
                    k.dma("gp", WOG[:].rearrange("p (c n) -> p c n", n=512), wosrc,
                          reads=[W1["Wo"]], writes=[WOG])
                    for c in range(16):
                        k.op("ve", lambda e, c=c: e.tensor_scalar(
                            out=WOG[:, c * 512:(c + 1) * 512],
                            in0=WOG[:, c * 512:(c + 1) * 512],
                            scalar1=GCOL[:, c % 2:c % 2 + 1], scalar2=None,
                            op0=ALU.mult), reads=[GCOL], writes=[WOG])
                    for qt in range(4):
                        pb = qt % 2
                        for c in range(16):
                            k.op("pe", lambda e, c=c, qt=qt, pb=pb: e.matmul(
                                pst[pb][:], OTG[:, c * 512 + qt * 128:c * 512 + (qt + 1) * 128],
                                WOG[:, c * 512:(c + 1) * 512],
                                start=(c == 0), stop=(c == 15)),
                                reads=[OTG, WOG], writes=[pst[pb]])
                        k.op("se", lambda e, pb=pb: e.copy(
                            out=AOS[pb][:], in_=pst[pb][:]), reads=[pst[pb]], writes=[AOS[pb]])
                        row0 = qg * 512 + qt * 128
                        k.dma("sy", y2_dram.t.ap()[row0:row0 + 128, dg * 512:(dg + 1) * 512],
                              AOS[pb][:], reads=[AOS[pb]], writes=[y2_dram])

        k.switch4(body)
        ph.close()
        ph2 = contextlib.ExitStack()

        def psb2(name, shape, dt=F32):
            return k.buf(ph2.enter_context(nc.sbuf_tensor(k.uname(name), list(shape), dt)), name)

        Y2T = [psb2("B1_Y2T0", [128, 2048]), psb2("B1_Y2T1", [128, 2048])]
        XT = [psb2("B1_XT0", [128, 2048]), psb2("B1_XT1", [128, 2048])]
        SQ = psb2("B1_SQe", [128, 2048])
        STAT = [psb2("B1_STe0", [128, 8]), psb2("B1_STe1", [128, 8])]
        Y1 = [psb2("B1_Y1e0", [128, 2048]), psb2("B1_Y1e1", [128, 2048])]
        for t in range(TT):
            b = t % 2
            k.dma("sy", Y2T[b][:], y2_dram.t.ap()[t * 128:(t + 1) * 128, :],
                  reads=[y2_dram], writes=[Y2T[b]])
            k.dma("sy", XT[b][:], x_out.t.ap()[t * 128:(t + 1) * 128, :],
                  reads=[x_out], writes=[XT[b]])
            emit_ln_norm(k, st, Y2T[b], SQ, STAT[b], epsc)
            k.op("ve", lambda e, b=b: e.tensor_scalar(
                out=Y1[b][:], in0=Y2T[b][:], scalar1=STAT[b][:, 6:7],
                scalar2=STAT[b][:, 7:8], op0=ALU.mult, op1=ALU.add),
                reads=[Y2T[b], STAT[b]], writes=[Y1[b]])
            k.op("ve", lambda e, b=b: e.tensor_tensor(
                out=XT[b][:], in0=XT[b][:], in1=Y1[b][:], op=ALU.add),
                reads=[Y1[b]], writes=[XT[b]])
            k.dma("sy", x_out.t.ap()[t * 128:(t + 1) * 128, :], XT[b][:],
                  reads=[XT[b]], writes=[x_out])
        k.barrier()
        ph2.close()

    def phase_prep(L):
        ph = contextlib.ExitStack()

        def psb(name, shape, dt=F32):
            return k.buf(ph.enter_context(nc.sbuf_tensor(k.uname(name), list(shape), dt)), name)

        def pps(name, shape, dt=F32):
            return k.buf(ph.enter_context(nc.psum_tensor(k.uname(name), list(shape), dt)), name)

        ps_tr = [pps("P_TR0", [128, 128], BF16), pps("P_TR1", [128, 128], BF16)]
        ps11 = pps("P_11", [1, 1])
        pcol = pps("P_COL", [128, 1])

        def quant_matrix(wsrc, nrow, ncol, dstT, dst_layout, px):
            nt = nrow // 128
            WT = [psb(px + "_WT0", [128, ncol]), psb(px + "_WT1", [128, ncol])]
            ACC = psb(px + "_ACC", [128, 1])
            TCOL = psb(px + "_TCOL", [128, 1])
            TAU = psb(px + "_TAU", [1, 1])
            TFACT = psb(px + "_TFACT", [1, 1])
            TAUC = psb(px + "_TAUC", [128, 1])
            NTAUC = psb(px + "_NTAUC", [128, 1])
            POS = psb(px + "_POS", [128, ncol], BF16)
            NEG = psb(px + "_NEG", [128, ncol], BF16)
            WQB = psb(px + "_WQB", [128, ncol], BF16)
            TSTG = psb(px + "_TSTG", [128, ncol], BF16)
            k.op("ve", lambda e: e.memset(ACC[:], 0.0), writes=[ACC])
            k.op("ve", lambda e: e.memset(TFACT[:], 0.7 / (nrow * ncol)), writes=[TFACT])
            wap = wsrc.t.ap()
            for t in range(nt):
                b = t % 2
                k.dma("sy", WT[b][:], wap[t * 128:(t + 1) * 128, :],
                      reads=[wsrc], writes=[WT[b]])
                k.op("ve", lambda e, b=b: e.reduce_sum(
                    out=TCOL[:], in_=WT[b][:], axis=AX.X, apply_absolute_value=True),
                    reads=[WT[b]], writes=[TCOL])
                k.op("ve", lambda e: e.tensor_tensor(
                    out=ACC[:], in0=ACC[:], in1=TCOL[:], op=ALU.add),
                    reads=[TCOL], writes=[ACC])
            k.op("pe", lambda e: e.matmul(ps11[:], ACC[:], consts_sb["ones_col_f"][:],
                                          start=True, stop=True),
                 reads=[ACC, consts_sb["ones_col_f"]], writes=[ps11])
            k.op("ve", lambda e: e.tensor_tensor(out=TAU[:], in0=ps11[:], in1=TFACT[:],
                                                 op=ALU.mult),
                 reads=[ps11, TFACT], writes=[TAU])
            k.op("pe", lambda e: e.matmul(pcol[:], consts_sb["ones_f"][:], TAU[:],
                                          start=True, stop=True),
                 reads=[consts_sb["ones_f"], TAU], writes=[pcol])
            k.op("ve", lambda e: e.tensor_copy(out=TAUC[:], in_=pcol[:]),
                 reads=[pcol], writes=[TAUC])
            k.op("ve", lambda e: e.tensor_scalar(out=NTAUC[:], in0=TAUC[:], scalar1=-1.0,
                                                 scalar2=None, op0=ALU.mult),
                 reads=[TAUC], writes=[NTAUC])
            nc_ = ncol // 128
            for t in range(nt):
                b = t % 2
                k.dma("sy", WT[b][:], wap[t * 128:(t + 1) * 128, :],
                      reads=[wsrc], writes=[WT[b]])
                k.op("ve", lambda e, b=b: e.tensor_scalar(
                    out=POS[:], in0=WT[b][:], scalar1=TAUC[:, 0:1], scalar2=None,
                    op0=ALU.is_ge), reads=[WT[b], TAUC], writes=[POS])
                k.op("ve", lambda e, b=b: e.tensor_scalar(
                    out=NEG[:], in0=WT[b][:], scalar1=NTAUC[:, 0:1], scalar2=None,
                    op0=ALU.is_le), reads=[WT[b], NTAUC], writes=[NEG])
                k.op("ve", lambda e: e.tensor_tensor(
                    out=WQB[:], in0=POS[:], in1=NEG[:], op=ALU.subtract),
                    reads=[POS, NEG], writes=[WQB])
                _transpose_out(t, WQB, TSTG, nc_, dstT, dst_layout, nt)
            return

        def _transpose_out(t, WQB, TSTG, nchunks, dstT, dst_layout, nt):
            for c in range(nchunks):
                pb = c % 2
                k.op("pe", lambda e, c=c, pb=pb: e.transpose(
                    ps_tr[pb][:], WQB[:, c * 128:(c + 1) * 128], consts_sb["ident_bf"][:]),
                    reads=[WQB, consts_sb["ident_bf"]], writes=[ps_tr[pb]])
                k.op("ve", lambda e, c=c, pb=pb: e.tensor_copy(
                    out=TSTG[:, c * 128:(c + 1) * 128], in_=ps_tr[pb][:]),
                    reads=[ps_tr[pb]], writes=[TSTG])
            if dst_layout == "up":
                dst = dstT.t.ap().rearrange("(dc rest) -> dc rest", rest=64 * 16384)[
                    :, t * 16384:(t + 1) * 16384].rearrange("dc (p n) -> p dc n", n=128)
                src = TSTG[:].rearrange("p (dc n) -> p dc n", n=128)
            else:
                dst = dstT.t.ap().rearrange("(fc rest) -> fc rest", rest=16 * 16384)[
                    :, t * 16384:(t + 1) * 16384].rearrange("fc (p n) -> p fc n", n=128)
                src = TSTG[:].rearrange("p (fc n) -> p fc n", n=128)
            k.dma("sy", dst, src, reads=[TSTG], writes=[dstT])

        def dequant_matrix(qsrc, scsrc, nrow, ncol, dstT, dst_layout, px):
            nt = nrow // 128
            ng = ncol // 64
            WI = [psb(px + "_WI0", [128, ncol], I32), psb(px + "_WI1", [128, ncol], I32)]
            WF = psb(px + "_WF", [128, ncol])
            SC = [psb(px + "_SC0", [128, ng]), psb(px + "_SC1", [128, ng])]
            WQB = psb(px + "_WQB", [128, ncol], BF16)
            TSTG = psb(px + "_TSTG", [128, ncol], BF16)
            qap = qsrc.t.ap()
            sap = scsrc.t.ap()
            for t in range(nt):
                b = t % 2
                k.dma("sy", WI[b][:], qap[t * 128:(t + 1) * 128, :],
                      reads=[qsrc], writes=[WI[b]])
                k.dma("sy", SC[b][:], sap[t * 128:(t + 1) * 128, :],
                      reads=[scsrc], writes=[SC[b]])
                k.op("se", lambda e, b=b: e.activation(
                    out=WF[:], in_=WI[b][:], func=AF.Copy, bias=-7.5, scale=1.0),
                    reads=[WI[b]], writes=[WF])
                k.op("ve", lambda e, b=b: e.tensor_tensor(
                    out=WQB[:].rearrange("p (g o) -> p g o", o=64),
                    in0=WF[:].rearrange("p (g o) -> p g o", o=64),
                    in1=SC[b][:].rearrange("p (g o) -> p g o", o=1).broadcast_to(
                        [128, ng, 64]),
                    op=ALU.mult), reads=[WF, SC[b]], writes=[WQB])
                _transpose_out(t, WQB, TSTG, ncol // 128, dstT, dst_layout, nt)

        if L == 0:
            quant_matrix(w1, FF, D, wupT, "up", "Pa")
            quant_matrix(w2, D, FF, wdnT, "dn", "Pb")
        else:
            dequant_matrix(wupq, wupsc, FF, D, wupT, "up", "Pa")
            dequant_matrix(wdnq, wdnsc, D, FF, wdnT, "dn", "Pb")
        k.barrier()
        ph.close()

    def phase_C(L):
        ph = contextlib.ExitStack()

        def psb(name, shape, dt=F32):
            return k.buf(ph.enter_context(nc.sbuf_tensor(k.uname(name), list(shape), dt)), name)

        def pps(name, shape, dt=F32):
            return k.buf(ph.enter_context(nc.psum_tensor(k.uname(name), list(shape), dt)), name)

        XN_T = psb("C_XN_T", [128, 16 * 2048], BF16)
        tmp = {
            "XT": [psb("C_XT0", [128, 2048]), psb("C_XT1", [128, 2048])],
            "SQ": psb("C_SQ", [128, 2048]),
            "STAT": [psb("C_ST0", [128, 8]), psb("C_ST1", [128, 8])],
            "XN": [psb("C_XN0", [128, 2048], BF16), psb("C_XN1", [128, 2048], BF16)],
            "PTR": [pps("C_PT0", [128, 128], BF16), pps("C_PT1", [128, 128], BF16)],
        }
        emit_xnT(k, st, x_out, XN_T, tmp, epsc, consts_sb)

        W1G = [psb("C_W1G0", [128, 16 * 512], BF16), psb("C_W1G1", [128, 16 * 512], BF16)]
        BUPR = psb("C_BUPR", [1, FF], BF16)
        X2S = [psb("C_X2S0", [128, 512], BF16), psb("C_X2S1", [128, 512], BF16)]
        X2TS = [psb("C_X2TS0", [128, 512], BF16), psb("C_X2TS1", [128, 512], BF16)]
        pu = [pps("C_PU0", [128, 512]), pps("C_PU1", [128, 512])]
        ps_tr = [pps("C_TR0", [128, 128], BF16), pps("C_TR1", [128, 128], BF16)]
        _bsrc = b1 if L == 0 else bup
        for _q in range(4):
            k.dma("sy", brow_f32[:], _bsrc.t.ap()[0:1, _q * 2048:(_q + 1) * 2048],
                  reads=[_bsrc], writes=[brow_f32])
            k.op("se", lambda e, _q=_q: e.copy(
                out=BUPR[0:1, _q * 2048:(_q + 1) * 2048], in_=brow_f32[:]),
                reads=[brow_f32], writes=[BUPR])
        if L == 1:
            LAU = psb("C_LAU", [128, 256], BF16)
            LBU = psb("C_LBU", [16, FF], BF16)
            TAT = psb("C_TAT", [16, 2048], BF16)
            pta = pps("C_PTA", [16, 512])
            k.dma("gp", LAU[:].rearrange("p (c n) -> p c n", n=16),
                  lau.t.ap().rearrange("(c p) n -> p c n", p=128), reads=[lau], writes=[LAU])
            k.dma("gp", LBU[:], lbu.t.ap(), reads=[lbu], writes=[LBU])
            for tg in range(4):
                for c in range(16):
                    k.op("pe", lambda e, c=c, tg=tg: e.matmul(
                        pta[:], LAU[:, c * 16:(c + 1) * 16],
                        XN_T[:, c * 2048 + tg * 512:c * 2048 + (tg + 1) * 512],
                        start=(c == 0), stop=(c == 15)),
                        reads=[LAU, XN_T], writes=[pta])
                k.op("ve", lambda e, tg=tg: e.tensor_copy(
                    out=TAT[0:16, tg * 512:(tg + 1) * 512], in_=pta[:]),
                    reads=[pta], writes=[TAT])
        act_fn = AF.Gelu if L == 0 else AF.Relu
        for f in range(16):
            wb = f % 2
            srcj = wupT.t.ap().rearrange("(dc rest) -> dc rest", rest=64 * 16384)[
                :, (4 * f) * 16384:(4 * f + 4) * 16384].rearrange(
                "dc (j p n) -> p dc j n", p=128, n=128)
            w1gv = W1G[wb][:].rearrange("p (dc jn) -> p dc jn", jn=512)
            for j in range(4):
                k.dma("sy", w1gv[:, :, j * 128:(j + 1) * 128], srcj[:, :, j, :],
                      reads=[wupT], writes=[W1G[wb]])
            for t in range(TT):
                pb = t % 2
                for c in range(16):
                    k.op("pe", lambda e, c=c, t=t, pb=pb, wb=wb: e.matmul(
                        pu[pb][:], XN_T[:, c * 2048 + t * 128:c * 2048 + (t + 1) * 128],
                        W1G[wb][:, c * 512:(c + 1) * 512],
                        start=(c == 0), stop=False),
                        reads=[XN_T, W1G[wb]], writes=[pu[pb]])
                if L == 1:
                    k.op("pe", lambda e, t=t, pb=pb, f=f: e.matmul(
                        pu[pb][:], TAT[0:16, t * 128:(t + 1) * 128],
                        LBU[0:16, f * 512:(f + 1) * 512], start=False, stop=False),
                        reads=[TAT, LBU], writes=[pu[pb]])
                k.op("pe", lambda e, pb=pb, f=f: e.matmul(
                    pu[pb][:], consts_sb["ones_bf"][:], BUPR[0:1, f * 512:(f + 1) * 512],
                    start=False, stop=True),
                    reads=[consts_sb["ones_bf"], BUPR], writes=[pu[pb]])
                k.op("se", lambda e, pb=pb: e.activation(
                    out=X2S[pb][:], in_=pu[pb][:], func=act_fn),
                    reads=[pu[pb]], writes=[X2S[pb]])
                for j in range(4):
                    tb = j % 2
                    k.op("pe", lambda e, j=j, pb=pb, tb=tb: e.transpose(
                        ps_tr[tb][:], X2S[pb][:, j * 128:(j + 1) * 128],
                        consts_sb["ident_bf"][:]),
                        reads=[X2S[pb], consts_sb["ident_bf"]], writes=[ps_tr[tb]])
                    k.op("ve", lambda e, j=j, pb=pb, tb=tb: e.tensor_copy(
                        out=X2TS[pb][:, j * 128:(j + 1) * 128], in_=ps_tr[tb][:]),
                        reads=[ps_tr[tb]], writes=[X2TS[pb]])
                dst = x2t.t.ap()[(t * 64 + 4 * f) * 16384:(t * 64 + 4 * f + 4) * 16384
                                 ].rearrange("(j p n) -> p j n", p=128, n=128)
                k.dma("sy", dst, X2TS[pb][:].rearrange("p (j n) -> p j n", n=128),
                      reads=[X2TS[pb]], writes=[x2t])
        k.barrier()
        ph.close()

    def phase_D(L):
        ph = contextlib.ExitStack()

        def psb(name, shape, dt=F32):
            return k.buf(ph.enter_context(nc.sbuf_tensor(k.uname(name), list(shape), dt)), name)

        def pps(name, shape, dt=F32):
            return k.buf(ph.enter_context(nc.psum_tensor(k.uname(name), list(shape), dt)), name)

        W2G = psb("D_W2G", [128, 64 * 512], BF16)
        X2TSB = [psb("D_X2T0", [128, FF], BF16), psb("D_X2T1", [128, FF], BF16)]
        Y2S = [psb("D_Y2S0", [128, 512]), psb("D_Y2S1", [128, 512])]
        BDNR = psb("D_BDNR", [1, D], BF16)
        BDNRF = psb("D_BDNRF", [1, D])
        _dsrc = b2 if L == 0 else bdn
        k.dma("sy", BDNRF[:], _dsrc.t.ap(), reads=[_dsrc], writes=[BDNRF])
        k.op("se", lambda e: e.copy(out=BDNR[:], in_=BDNRF[:]),
             reads=[BDNRF], writes=[BDNR])
        pd = [pps("D_PD0", [128, 512]), pps("D_PD1", [128, 512])]
        if L == 1:
            LAD = psb("D_LAD", [128, 1024], BF16)
            LBD = psb("D_LBD", [16, 2048], BF16)
            TDT = psb("D_TDT", [16, 2048], BF16)
            ptd = pps("D_PTD", [16, 128])
            k.dma("gp", LAD[:].rearrange("p (c n) -> p c n", n=16),
                  lad.t.ap().rearrange("(c p) n -> p c n", p=128), reads=[lad], writes=[LAD])
            k.dma("gp", LBD[:], lbd.t.ap(), reads=[lbd], writes=[LBD])
        for dg in range(4):
            wsrc = wdnT.t.ap().rearrange("(fc rest) -> fc rest", rest=16 * 16384)[
                :, (4 * dg) * 16384:(4 * dg + 4) * 16384].rearrange(
                "fc (j p n) -> p fc j n", p=128, n=128)
            w2gv = W2G[:].rearrange("p (fc jn) -> p fc jn", jn=512)
            for q4 in range(4):
                for j in range(4):
                    k.dma("sy", w2gv[:, q4 * 16:(q4 + 1) * 16, j * 128:(j + 1) * 128],
                          wsrc[:, q4 * 16:(q4 + 1) * 16, j, :], reads=[wdnT], writes=[W2G])
            for t in range(TT):
                pb = t % 2
                k.dma("sy", X2TSB[pb][:].rearrange("p (fc n) -> p fc n", n=128),
                      x2t.t.ap()[t * 64 * 16384:(t + 1) * 64 * 16384].rearrange(
                          "(fc p n) -> p fc n", p=128, n=128),
                      reads=[x2t], writes=[X2TSB[pb]])
                if L == 1 and dg == 0:
                    for fc in range(64):
                        k.op("pe", lambda e, fc=fc, pb=pb: e.matmul(
                            ptd[:], LAD[:, fc * 16:(fc + 1) * 16],
                            X2TSB[pb][:, fc * 128:(fc + 1) * 128],
                            start=(fc == 0), stop=(fc == 63)),
                            reads=[LAD, X2TSB[pb]], writes=[ptd])
                    k.op("ve", lambda e, t=t: e.tensor_copy(
                        out=TDT[0:16, t * 128:(t + 1) * 128], in_=ptd[:]),
                        reads=[ptd], writes=[TDT])
                for fc in range(64):
                    k.op("pe", lambda e, fc=fc, t=t, pb=pb, dg=dg: e.matmul(
                        pd[pb][:], X2TSB[pb][:, fc * 128:(fc + 1) * 128],
                        W2G[:, fc * 512:(fc + 1) * 512],
                        start=(fc == 0), stop=False),
                        reads=[X2TSB[pb], W2G], writes=[pd[pb]])
                if L == 1:
                    k.op("pe", lambda e, t=t, pb=pb, dg=dg: e.matmul(
                        pd[pb][:], TDT[0:16, t * 128:(t + 1) * 128],
                        LBD[0:16, dg * 512:(dg + 1) * 512], start=False, stop=False),
                        reads=[TDT, LBD], writes=[pd[pb]])
                k.op("pe", lambda e, pb=pb, dg=dg: e.matmul(
                    pd[pb][:], consts_sb["ones_bf"][:], BDNR[0:1, dg * 512:(dg + 1) * 512],
                    start=False, stop=True),
                    reads=[consts_sb["ones_bf"], BDNR], writes=[pd[pb]])
                k.op("se", lambda e, pb=pb: e.copy(out=Y2S[pb][:], in_=pd[pb][:]),
                     reads=[pd[pb]], writes=[Y2S[pb]])
                k.dma("sy", y2_dram.t.ap()[t * 128:(t + 1) * 128, dg * 512:(dg + 1) * 512],
                      Y2S[pb][:], reads=[Y2S[pb]], writes=[y2_dram])
        Y2T = [psb("D_Y2T0", [128, 2048]), psb("D_Y2T1", [128, 2048])]
        XT = [psb("D_XT0", [128, 2048]), psb("D_XT1", [128, 2048])]
        SQ = psb("D_SQ", [128, 2048])
        STAT = [psb("D_ST0", [128, 8]), psb("D_ST1", [128, 8])]
        Y1 = [psb("D_Y10", [128, 2048]), psb("D_Y11", [128, 2048])]
        for t in range(TT):
            b = t % 2
            k.dma("sy", Y2T[b][:], y2_dram.t.ap()[t * 128:(t + 1) * 128, :],
                  reads=[y2_dram], writes=[Y2T[b]])
            k.dma("sy", XT[b][:], x_out.t.ap()[t * 128:(t + 1) * 128, :],
                  reads=[x_out], writes=[XT[b]])
            emit_ln_norm(k, st, Y2T[b], SQ, STAT[b], epsc)
            k.op("ve", lambda e, b=b: e.tensor_scalar(
                out=Y1[b][:], in0=Y2T[b][:], scalar1=STAT[b][:, 6:7],
                scalar2=STAT[b][:, 7:8], op0=ALU.mult, op1=ALU.add),
                reads=[Y2T[b], STAT[b]], writes=[Y1[b]])
            k.op("ve", lambda e, b=b: e.tensor_tensor(
                out=XT[b][:], in0=XT[b][:], in1=Y1[b][:], op=ALU.add),
                reads=[Y1[b]], writes=[XT[b]])
            k.dma("sy", x_out.t.ap()[t * 128:(t + 1) * 128, :], XT[b][:],
                  reads=[XT[b]], writes=[x_out])
        k.barrier()
        ph.close()

    phase_A(0)
    if stop_after == "L0A":
        pass
    elif stop_after is not None and stop_after.startswith("L0Bsub:"):
        _, spec = stop_after.split(":", 1)
        qn, hn, up = spec.split(",")
        phase_B0(nqg=int(qn), nh=int(hn), upto=up)
    else:
        phase_B0()
        if stop_after != "L0B":
            phase_prep(0)
            phase_C(0)
            phase_D(0)
            if stop_after != "L0D":
                phase_A(1)
                phase_B1()
                if stop_after != "L1B":
                    phase_prep(1)
                    phase_C(1)
                    phase_D(1)

    for nm, (src, dst) in dbg_bufs.items():
        k.dma("sy", dst.t.ap(), src.t.ap(), reads=[src], writes=[dst])
    k.finish()
    prog_stack.close()
    return nc


def _rope_tables(positions):
    j = np.arange(64, dtype=np.float32)
    inv_freq = np.power(np.float32(10000.0), (-2.0 * j / 128.0).astype(np.float32))
    pos = positions.astype(np.float32)
    ang = pos[:, None] * inv_freq[None, :]
    sin = np.sin(ang).astype(np.float32)
    cos = np.cos(ang).astype(np.float32)
    sin2 = np.stack([sin, sin], axis=-1).reshape(len(positions), 128)
    cos2 = np.stack([cos, cos], axis=-1).reshape(len(positions), 128)
    return cos2.T.copy(), sin2.T.copy()


def core_positions(c):
    c4 = c % 4
    out = []
    for jz in zeb(c4):
        out.extend(range(jz * 512, (jz + 1) * 512))
    return np.array(out, dtype=np.int64)


def kernel(**inputs):
    nc = build_program()
    in_maps = []
    for c in range(8):
        b = c // 4
        pos = core_positions(c)
        cos, sin = _rope_tables(pos)
        m = {
            "x_in": np.ascontiguousarray(inputs["L0_attn_hidden_state"][b, pos, :]),
            "rope_cos": cos, "rope_sin": sin,
            "L0_Wq": inputs["L0_attn_Wq"], "L0_Wk": inputs["L0_attn_Wk"],
            "L0_Wv": inputs["L0_attn_Wv"], "L0_Wo": inputs["L0_attn_Wo"],
            "L0_bq": inputs["L0_attn_bq"].reshape(1, -1),
            "L0_bk": inputs["L0_attn_bk"].reshape(1, -1),
            "L0_bv": inputs["L0_attn_bv"].reshape(1, -1),
            "L0_bo": inputs["L0_attn_bo"].reshape(1, -1),
            "L0_w1": inputs["L0_ffn_w1"], "L0_b1": inputs["L0_ffn_b1"].reshape(1, -1),
            "L0_w2": inputs["L0_ffn_w2"], "L0_b2": inputs["L0_ffn_b2"].reshape(1, -1),
            "L1_Wq": inputs["L1_attn_Wq"], "L1_Wk": inputs["L1_attn_Wk"],
            "L1_Wv": inputs["L1_attn_Wv"], "L1_Wo": inputs["L1_attn_Wo"],
            "L1_lamv": np.concatenate([
                inputs["L1_attn_lambda_q1"], inputs["L1_attn_lambda_k1"],
                inputs["L1_attn_lambda_q2"], inputs["L1_attn_lambda_k2"]]).reshape(1, -1),
            "L1_g": inputs["L1_attn_g"].reshape(1, -1),
            "L1_wupq": inputs["L1_ffn_w_up_q"], "L1_wupsc": inputs["L1_ffn_w_up_scale"],
            "L1_bup": inputs["L1_ffn_b_up"].reshape(1, -1),
            "L1_lau": inputs["L1_ffn_w_up_lora_a"], "L1_lbu": inputs["L1_ffn_w_up_lora_b"],
            "L1_wdnq": inputs["L1_ffn_w_down_q"], "L1_wdnsc": inputs["L1_ffn_w_down_scale"],
            "L1_bdn": inputs["L1_ffn_b_down"].reshape(1, -1),
            "L1_lad": inputs["L1_ffn_w_down_lora_a"], "L1_lbd": inputs["L1_ffn_w_down_lora_b"],
        }
        in_maps.append({kk: np.ascontiguousarray(np.asarray(vv)) for kk, vv in m.items()})
    res = run_bass_kernel_spmd(nc, in_maps, core_ids=list(range(8)))
    out = np.zeros((2, S, D), dtype=np.float32)
    for c in range(8):
        out[c // 4, core_positions(c), :] = res.results[c]["x_out"]
    return out
